# version 21
# baseline (speedup 1.0000x reference)
import contextlib
import numpy as np
import concourse.bass as bass
import concourse.mybir as mybir
from concourse.bass_utils import run_bass_kernel_spmd

F32 = mybir.dt.float32
BF16 = mybir.dt.bfloat16
U8 = mybir.dt.uint8
AF = mybir.ActivationFunctionType
OP = mybir.AluOpType
AX = mybir.AxisListType.X

TAPS_WANT = ()
SERIAL = False
SERIAL_FROM = 0
SERIAL_TO = 10**9
ONLY_LOAD = False
PHASE_MARKS = {}

C = 1024
NCH = 8
HALO = 256
REAL = 2048
EXT = HALO + REAL
EPS = 1e-5
LOGMASK = -30000.0

STS = [
    dict(x0=0, x1=1280, dom=128, gdom=256,
         kvr=[(0, 256), (256, 512), (512, 768), (768, 1024),
              (1024, 1280)],
         gt=[(128, 256), (256, 768), (768, 1280)],
         rt=[(256, 768), (768, 1280)]),
    dict(x0=1280, x1=2304, dom=1280, gdom=1280,
         kvr=[(1280, 1536), (1536, 1792), (1792, 2048), (2048, 2304)],
         gt=[(1280, 1792), (1792, 2304)],
         rt=[(1280, 1792), (1792, 2304)]),
]


class Buf:
    def __init__(self, handle, track, byte_off, ebytes, cols):
        self.h = handle
        self.track = track
        self.off = byte_off
        self.eb = ebytes
        self.cols = cols

    def iv(self, lo=0, hi=None):
        if hi is None:
            hi = self.cols
        return (self.track, self.off + lo * self.eb, self.off + hi * self.eb)


class Prog:
    ENGS = ("tensor", "vector", "scalar", "gpsimd", "sync")

    def __init__(self):
        self.ops = []
        self.hist = {}

    def emit(self, eng, fn, reads=(), writes=(), dma=None, tag=""):
        i = len(self.ops)
        deps = set()
        for (t, lo, hi) in reads:
            h = self.hist.get(t)
            if h:
                for (l2, h2, j) in h["w"]:
                    if l2 < hi and lo < h2:
                        deps.add(j)
        for (t, lo, hi) in writes:
            h = self.hist.get(t)
            if h:
                lw = -1
                for (l2, h2, j) in h["w"]:
                    if l2 < hi and lo < h2:
                        deps.add(j)
                        lw = max(lw, j)
                for (l2, h2, j) in h["r"]:
                    if l2 < hi and lo < h2 and j > lw:
                        deps.add(j)
        self.ops.append({"eng": eng, "fn": fn, "deps": deps, "dma": dma,
                         "tag": tag, "need_inc": dma is not None})
        for (t, lo, hi) in reads:
            self.hist.setdefault(t, {"w": [], "r": []})["r"].append((lo, hi, i))
        for (t, lo, hi) in writes:
            self.hist.setdefault(t, {"w": [], "r": []})["w"].append((lo, hi, i))
        return i

    def finalize(self):
        ops = self.ops
        if SERIAL:
            for i, op in enumerate(ops):
                if SERIAL_FROM < i < SERIAL_TO:
                    op["deps"].add(i - 1)
        for op in ops:
            for j in op["deps"]:
                p = ops[j]
                if p["eng"] == op["eng"] == "tensor" and p["dma"] is None:
                    continue
                p["need_inc"] = True
        ecnt = {e: 0 for e in self.ENGS}
        dcnt = {}
        for op in ops:
            if op["dma"] is not None:
                k = op["dma"]
                dcnt[k] = dcnt.get(k, 0) + 1
                op["inc_val"] = dcnt[k] * 16
            elif op["need_inc"]:
                ecnt[op["eng"]] += 1
                op["inc_val"] = ecnt[op["eng"]]
        pref = {}
        run = {}
        for i, op in enumerate(ops):
            pref[i] = None
            if op["dma"] is not None:
                run[op["dma"]] = run.get(op["dma"], 0) + 1
                op["kcount"] = run[op["dma"]]
        for i, op in enumerate(ops):
            need = {}
            for j in op["deps"]:
                p = ops[j]
                if p["dma"] is not None:
                    key = ("d", p["dma"])
                    v = 16 * self._emitted_before(ops, p["dma"], i)
                elif p["eng"] == op["eng"] == "tensor":
                    continue
                else:
                    key = ("e", p["eng"])
                    v = p["inc_val"]
                if need.get(key, 0) < v:
                    need[key] = v
            op["waits"] = need
        self.dtot = dcnt
        return sorted(dcnt)

    def _emitted_before(self, ops, key, idx):
        cache = getattr(self, "_ebcache", None)
        if cache is None or cache[0] is not ops:
            pc = {}
            counts = {}
            for i, op in enumerate(ops):
                pc[i] = dict(counts) if op["deps"] else None
                if op["dma"] is not None:
                    counts[op["dma"]] = counts.get(op["dma"], 0) + 1
            self._ebcache = (ops, pc, counts)
        pc = self._ebcache[1][idx]
        if pc is None:
            pc = {}
        return pc.get(key, 0)

    def replay(self, eng_name, eng_obj, esems, dsems, guaranteed):
        for op in self.ops:
            if op["eng"] != eng_name:
                continue
            g = guaranteed[eng_name]
            for (kind, key), v in sorted(op["waits"].items()):
                if g.get((kind, key), 0) >= v:
                    continue
                eng_obj.wait_ge(dsems[key] if kind == "d" else esems[key], v)
                g[(kind, key)] = v
            inst = op["fn"](eng_obj)
            if op["dma"] is not None:
                inst.then_inc(dsems[op["dma"]], 16)
            elif op["need_inc"]:
                inst.then_inc(esems[eng_name], 1)


def build_nc():
    nc = bass.Bass("TRN2", target_bir_lowering=False)

    xsl = nc.dram_tensor("xsl", [EXT, C], F32, kind="ExternalInput").ap()
    mrow = nc.dram_tensor("mrow", [1, HALO], F32, kind="ExternalInput").ap()
    lrow = nc.dram_tensor("lrow", [1, HALO], F32, kind="ExternalInput").ap()
    wdram = {}
    for nm in ["Wk", "Wv", "Wr", "Wo", "Wcr"]:
        wdram[nm] = nc.dram_tensor(nm, [128, 8192], F32,
                                   kind="ExternalInput").ap()
    for nm in ["Wck", "Wcv"]:
        wdram[nm] = nc.dram_tensor(nm, [4, 128, 8192], F32,
                                   kind="ExternalInput").ap()
    wffn = nc.dram_tensor("Wffn", [8, 128, 8192], F32,
                          kind="ExternalInput").ap()
    bffn = nc.dram_tensor("bffn", [8, C], F32, kind="ExternalInput").ap()
    sigs = nc.dram_tensor("sigs", [8, C], F32, kind="ExternalInput").ap()
    vec_in = {}
    for nm in ["ln0_g", "ln0_b", "ln1_g", "ln1_b", "time_decay", "time_first",
               "tm_k", "tm_v", "tm_r", "cm_k", "cm_r"]:
        vec_in[nm] = nc.dram_tensor(nm, [C], F32, kind="ExternalInput").ap()
    osl = nc.dram_tensor("osl", [REAL, C], F32, kind="ExternalOutput").ap()

    def alloc(name, shape, dt):
        h = nc.alloc_sbuf_tensor(name, shape, dt)
        eb = {BF16: 2, U8: 1}.get(dt, 4)
        return Buf(h, name, 0, eb, shape[-1])

    xaT = [alloc(f"xaT{k}", [128, 1280], BF16) for k in range(NCH)]
    XN = [alloc(f"XN{k}", [128, 1152], BF16) for k in range(NCH)]
    rwk = [alloc(f"rwk{k}", [128, 1152], BF16) for k in range(NCH)]
    xcS = [alloc(f"xcS{k}", [128, 1152], BF16) for k in range(NCH)]
    gS = [alloc(f"gS{k}", [128, 1024], BF16) for k in range(NCH)]
    VN = ["ln0_g", "ln0_b", "ln1_g", "ln1_b", "tm_k", "tm_v", "tm_r",
          "cm_k", "cm_r", "time_decay", "time_first", "ew", "eu"]
    vecs = {nm: alloc("v_" + nm, [128, 8], F32) for nm in VN}
    sgnT = alloc("sgnT", [128, 64], BF16)
    idn = alloc("idn", [128, 128], F32)
    m01b = alloc("m01b", [128, HALO], BF16)
    m01u = alloc("m01u", [128, HALO], U8)
    lmB = alloc("lmB", [128, HALO], F32)
    bffn_sb = alloc("bffn_sb", [8, C], BF16)
    selb = alloc("selb", [8, 1024], BF16)
    ones_f = alloc("ones_f", [1, 128], F32)
    onescol = alloc("onescol", [128, 1], F32)
    onescol_b = alloc("onescol_b", [128, 1], BF16)
    epscol = alloc("epscol", [128, 1], F32)
    tinycol = alloc("tinycol", [128, 1], F32)
    zerot = alloc("zerot", [128, 128], BF16)
    Ast = alloc("Ast", [128, 8], F32)
    Bst = alloc("Bst", [128, 8], F32)
    xacar = alloc("xacar", [128, 8], BF16)
    xccar = alloc("xccar", [128, 8], BF16)

    AR_F32 = 15744
    ar_h = nc.alloc_sbuf_tensor("arena", [128, AR_F32], F32)
    ar16 = ar_h.bitcast(BF16)
    ar8 = ar_h.bitcast(U8)
    NSLOT = 3
    WPOOL = [alloc(f"wp{i}", [128, 8192], BF16) for i in range(NSLOT)]

    class Arena:
        def __init__(self):
            self.off = 0

        def take(self, cols, dt):
            eb = {BF16: 2, U8: 1}.get(dt, 4)
            self.off = (self.off + 63) & ~63
            o = self.off
            self.off += cols * eb
            assert self.off <= AR_F32 * 4, f"arena overflow {self.off}"
            if dt == F32:
                h = ar_h[:, o // 4:o // 4 + cols]
            elif dt == U8:
                h = ar8[:, o:o + cols]
            else:
                h = ar16[:, o // 2:o // 2 + cols]
            return Buf(h, "ARENA", o, eb, cols)

    arena = Arena()
    pb = [Buf(nc.alloc_psum_tensor(f"pbank{i}", [128, 512], F32),
              f"pbank{i}", 0, 4, 512) for i in range(8)]

    P = Prog()
    E = P.emit

    def piv(b):
        return [b.iv(0, 512)]

    tw = list(TAPS_WANT)
    dbg = None
    if tw:
        dbg = nc.dram_tensor("dbg", [128, 516 * len(tw)], F32,
                             kind="ExternalOutput").ap()

    def tap(tag, ap, ivs):
        if tag not in tw:
            return
        i = tw.index(tag)
        p_, w_ = ap.shape[0], ap.shape[1]
        dst = dbg[0:p_, i * 516:i * 516 + w_]
        E("gpsimd", (lambda a_, d_: lambda e: e.dma_start(out=d_, in_=a_))(ap, dst),
          reads=list(ivs), writes=[], dma="tapq")

    it_a = arena.take(128, F32)
    it_b = arena.take(128, F32)
    vstage = arena.take(128, F32)
    sstage = arena.take(1024, F32)
    mstage = arena.take(HALO, F32)
    spi = arena.take(1024, F32)
    m01f = arena.take(HALO, F32)

    E("gpsimd", lambda e: e.iota(it_a.h, pattern=[[0, 128]], channel_multiplier=1,
                                 allow_small_or_imprecise_dtypes=True),
      writes=[it_a.iv()])
    E("gpsimd", lambda e: e.iota(it_b.h, pattern=[[1, 128]], channel_multiplier=0,
                                 allow_small_or_imprecise_dtypes=True),
      writes=[it_b.iv()])
    E("vector", lambda e: e.tensor_tensor(idn.h[:], it_a.h, it_b.h, OP.is_equal),
      reads=[it_a.iv(), it_b.iv()], writes=[idn.iv()])
    for pair in [(ones_f, 1.0), (onescol, 1.0), (onescol_b, 1.0), (zerot, 0.0),
                 (Ast, 0.0), (Bst, 0.0), (epscol, EPS), (tinycol, 1e-30)]:
        E("gpsimd", (lambda bb, vv: lambda e: e.memset(bb.h[:], vv))(*pair),
          writes=[pair[0].iv()])

    def init_vec(nm):
        dst = vecs[nm]
        src = vec_in[nm].rearrange("(c p) -> c p", p=128)
        E("sync", lambda e: e.dma_start(out=vstage.h[0:8, 0:128], in_=src),
          writes=[vstage.iv()], dma="vst")
        E("tensor", lambda e: e.transpose(pb[0].h[0:128, 0:8],
                                          vstage.h[0:8, 0:128],
                                          idn.h[0:8, 0:8]),
          reads=[vstage.iv(), idn.iv()], writes=piv(pb[0]))
        E("vector", lambda e: e.tensor_copy(dst.h[:], pb[0].h[0:128, 0:8]),
          reads=piv(pb[0]), writes=[dst.iv()])

    for nm in ["ln0_g", "ln0_b", "ln1_g", "ln1_b", "tm_k", "tm_v", "tm_r",
               "cm_k", "cm_r", "time_decay", "time_first"]:
        init_vec(nm)
    td, tf = vecs["time_decay"], vecs["time_first"]
    E("scalar", lambda e: e.activation(td.h[:], td.h[:], AF.Exp),
      reads=[td.iv()], writes=[td.iv()])
    E("scalar", lambda e: e.activation(vecs["ew"].h[:], td.h[:], AF.Exp, scale=-1.0),
      reads=[td.iv()], writes=[vecs["ew"].iv()])
    E("scalar", lambda e: e.activation(vecs["eu"].h[:], tf.h[:], AF.Exp),
      reads=[tf.iv()], writes=[vecs["eu"].iv()])

    E("gpsimd", lambda e: e.iota(spi.h[0:8, 0:1024], pattern=[[0, 1024]],
                                 channel_multiplier=1,
                                 allow_small_or_imprecise_dtypes=True),
      writes=[spi.iv()])
    E("gpsimd", lambda e: e.iota(sstage.h[0:8, 0:1024], pattern=[[1, 8], [0, 128]],
                                 channel_multiplier=0,
                                 allow_small_or_imprecise_dtypes=True),
      writes=[sstage.iv()])
    E("vector", lambda e: e.tensor_tensor(selb.h[:], spi.h[0:8, 0:1024],
                                          sstage.h[0:8, 0:1024], OP.is_equal),
      reads=[spi.iv(), sstage.iv()], writes=[selb.iv()])

    E("sync", lambda e: e.dma_start(out=sstage.h[0:8, 0:1024], in_=sigs),
      writes=[sstage.iv()], dma="sgq")
    E("scalar", lambda e: e.activation(sstage.h[0:8, 0:1024],
                                       sstage.h[0:8, 0:1024], AF.Sign),
      reads=[sstage.iv()], writes=[sstage.iv()])

    def init_sgn(kc):
        E("tensor", lambda e: e.transpose(pb[1].h[0:128, 0:8],
                                          sstage.h[0:8, kc * 128:(kc + 1) * 128],
                                          idn.h[0:8, 0:8]),
          reads=[sstage.iv(), idn.iv()], writes=piv(pb[1]))
        E("vector", lambda e: e.tensor_copy(sgnT.h[:, kc * 8:(kc + 1) * 8],
                                            pb[1].h[0:128, 0:8]),
          reads=piv(pb[1]), writes=[sgnT.iv(kc * 8, kc * 8 + 8)])

    for kc in range(NCH):
        init_sgn(kc)

    E("gpsimd", lambda e: e.dma_start(out=bffn_sb.h[:], in_=bffn),
      writes=[bffn_sb.iv()], dma="bffn")

    E("sync", lambda e: e.dma_start(out=mstage.h[0:1, 0:HALO], in_=mrow),
      writes=[mstage.iv()], dma="msq")
    E("tensor", lambda e: e.matmul(pb[2].h[:, 0:HALO], ones_f.h[:],
                                   mstage.h[0:1, 0:HALO], start=True, stop=True),
      reads=[mstage.iv(), ones_f.iv()], writes=piv(pb[2]))
    E("vector", lambda e: e.tensor_copy(m01f.h[:], pb[2].h[:, 0:HALO]),
      reads=piv(pb[2]), writes=[m01f.iv()])
    E("sync", lambda e: e.dma_start(out=mstage.h[0:1, 0:HALO], in_=lrow),
      writes=[mstage.iv()], dma="msq")
    E("tensor", lambda e: e.matmul(pb[3].h[:, 0:HALO], ones_f.h[:],
                                   mstage.h[0:1, 0:HALO], start=True, stop=True),
      reads=[mstage.iv(), ones_f.iv()], writes=piv(pb[3]))
    E("vector", lambda e: e.tensor_copy(lmB.h[:], pb[3].h[:, 0:HALO]),
      reads=piv(pb[3]), writes=[lmB.iv()])
    E("vector", lambda e: e.tensor_copy(m01b.h[:], m01f.h[:]),
      reads=[m01f.iv()], writes=[m01b.iv()])
    E("vector", lambda e: e.tensor_copy(m01u.h[:], m01f.h[:]),
      reads=[m01f.iv()], writes=[m01u.iv()])

    wrot = [0]

    def wload(src):
        i = wrot[0] % NSLOT
        wrot[0] += 1
        s = WPOOL[i]
        E("gpsimd", (lambda ss, sap: lambda e: e.dma_start(out=ss.h[:, 0:8192],
                                                           in_=sap))(s, src),
          writes=[s.iv(0, 8192)], dma=f"wp{i}")
        return s

    global PHASE_MARKS
    PHASE_MARKS = {}

    def mark(name):
        PHASE_MARKS[name] = len(P.ops)

    def do_st(sti, st):
        x0, dom, gdom = st["x0"], st["dom"], st["gdom"]
        nx = st["x1"] - x0

        arena.off = 0
        xt = [arena.take(1024, F32) for _ in range(6)]
        sql = [arena.take(1024, F32) for _ in range(3)]
        xnt = [arena.take(1024, F32) for _ in range(3)]
        stc = [[arena.take(1, F32) for _ in range(5)] for _ in range(3)]

        def load_dma(ti):
            r0 = x0 + ti * 128
            xb = xt[ti % 6]
            E("sync", lambda e: e.dma_start(out=xb.h[:], in_=xsl[r0:r0 + 128, :]),
              writes=[xb.iv()], dma=f"xt{ti % 6}")

        def load_tile(ti):
            r0 = x0 + ti * 128
            par = ti % 3
            xb, sq, xh = xt[ti % 6], sql[par], xnt[par]
            ssq, m, v, rst, nmr = stc[par]
            E("scalar", lambda e: e.activation(out=sq.h, in_=xb.h, func=AF.Square,
                                               accum_out=ssq.h[:]),
              reads=[xb.iv()], writes=[sq.iv(), ssq.iv()])
            E("vector", lambda e: e.reduce_sum(m.h, xb.h, axis=AX),
              reads=[xb.iv()], writes=[m.iv()])
            E("vector", lambda e: e.tensor_scalar(m.h, m.h, 1.0 / C, None,
                                                  op0=OP.mult),
              reads=[m.iv()], writes=[m.iv()])
            E("vector", lambda e: e.tensor_tensor(nmr.h, m.h, m.h, OP.mult),
              reads=[m.iv()], writes=[nmr.iv()])
            E("vector", lambda e: e.scalar_tensor_tensor(
                v.h, ssq.h, 1.0 / C, nmr.h, op0=OP.mult, op1=OP.subtract),
              reads=[ssq.iv(), nmr.iv()], writes=[v.iv()])
            E("scalar", lambda e: e.activation(v.h, v.h, AF.Ln, bias=epscol.h[:]),
              reads=[v.iv(), epscol.iv()], writes=[v.iv()])
            E("scalar", lambda e: e.activation(rst.h, v.h, AF.Exp, scale=-0.5),
              reads=[v.iv()], writes=[rst.iv()])
            E("vector", lambda e: e.scalar_tensor_tensor(
                nmr.h, m.h, -1.0, rst.h, op0=OP.mult, op1=OP.mult),
              reads=[m.iv(), rst.iv()], writes=[nmr.iv()])
            E("vector", lambda e: e.tensor_scalar(
                xh.h, xb.h, rst.h[:], nmr.h[:], op0=OP.mult, op1=OP.add),
              reads=[xb.iv(), rst.iv(), nmr.iv()], writes=[xh.iv()])
            E("scalar", lambda e: e.activation(out=sq.h, in_=xh.h, func=AF.Square,
                                               accum_out=ssq.h[:]),
              reads=[xh.iv()], writes=[sq.iv(), ssq.iv()])
            E("vector", lambda e: e.tensor_scalar(v.h, ssq.h, 1.0 / C, None,
                                                  op0=OP.mult),
              reads=[ssq.iv()], writes=[v.iv()])
            E("scalar", lambda e: e.activation(v.h, v.h, AF.Ln, bias=epscol.h[:]),
              reads=[v.iv(), epscol.iv()], writes=[v.iv()])
            E("scalar", lambda e: e.activation(rst.h, v.h, AF.Exp, scale=-0.5),
              reads=[v.iv()], writes=[rst.iv()])

            def tile_kc(kc):
                if r0 >= dom:
                    pbx = pb[kc % 2]
                    E("tensor", lambda e: e.transpose(
                        pbx.h[:, 0:128], xh.h[:, kc * 128:(kc + 1) * 128],
                        idn.h[:]),
                      reads=[xh.iv(), idn.iv()], writes=piv(pbx))
                    E("vector", lambda e: e.tensor_copy(
                        XN[kc].h[:, r0 - dom:r0 - dom + 128], pbx.h[:, 0:128]),
                      reads=piv(pbx),
                      writes=[XN[kc].iv(r0 - dom, r0 - dom + 128)])
                pby = pb[4 + kc % 2]
                E("scalar", lambda e: e.activation(
                    sq.h[:, kc * 128:(kc + 1) * 128],
                    xh.h[:, kc * 128:(kc + 1) * 128], AF.Copy, scale=rst.h[:]),
                  reads=[xh.iv(), rst.iv()],
                  writes=[sq.iv(kc * 128, kc * 128 + 128)])
                E("tensor", lambda e: e.transpose(
                    pby.h[:, 0:128], sq.h[:, kc * 128:(kc + 1) * 128], idn.h[:]),
                  reads=[sq.iv(kc * 128, kc * 128 + 128), idn.iv()],
                  writes=piv(pby))
                E("vector", lambda e: e.tensor_scalar(
                    xaT[kc].h[:, r0 - x0:r0 - x0 + 128], pby.h[:, 0:128],
                    vecs["ln0_g"].h[:, kc:kc + 1],
                    vecs["ln0_b"].h[:, kc:kc + 1], op0=OP.mult, op1=OP.add),
                  reads=piv(pby) + [vecs["ln0_g"].iv(), vecs["ln0_b"].iv()],
                  writes=[xaT[kc].iv(r0 - x0, r0 - x0 + 128)])
                if sti == 0 and r0 < HALO:
                    E("vector", lambda e: e.tensor_tensor(
                        xaT[kc].h[:, r0 - x0:r0 - x0 + 128],
                        xaT[kc].h[:, r0 - x0:r0 - x0 + 128],
                        m01b.h[:, r0:r0 + 128], OP.mult),
                      reads=[xaT[kc].iv(r0 - x0, r0 - x0 + 128), m01b.iv()],
                      writes=[xaT[kc].iv(r0 - x0, r0 - x0 + 128)])

            for kc in range(NCH):
                tile_kc(kc)

        for ti in range(min(6, nx // 128)):
            load_dma(ti)
        for ti in range(nx // 128):
            load_tile(ti)
            if ti + 6 < nx // 128:
                load_dma(ti + 6)
        if sti == 0:
            tap("xn0", XN[0].h[:, 0:512], [XN[0].iv(0, 512)])
            tap("xa0", xaT[0].h[:, 0:512], [xaT[0].iv(0, 512)])
        if sti == 0:
            def xa_car(kc):
                E("vector", lambda e: e.tensor_copy(
                    xacar.h[:, kc:kc + 1], xaT[kc].h[:, 1279:1280]),
                  reads=[xaT[kc].iv(1279, 1280)], writes=[xacar.iv(kc, kc + 1)])
            for kc in range(NCH):
                xa_car(kc)

        if ONLY_LOAD:
            return
        mark(f"ffn{sti}")
        arena.off = 0
        outac = [Buf(rwk[mc].h[:, 0:1024], f"rwk{mc}", 0, 2, 1024)
                 for mc in range(NCH)]
        M8t = [arena.take(512, BF16) for _ in range(2)]
        mbu = arena.take(16 * 512, U8)
        scsb = arena.take(512, F32)
        scT = arena.take(8, F32)
        mxc = arena.take(1, F32)
        mTf = arena.take(8, F32)
        sil = [arena.take(512, F32) for _ in range(2)]
        fb = st["rt"][0][0] - dom
        ntile = len(st["rt"])

        def ffn_scores(tix, t0, t1):
            L = t1 - t0
            xb = fb + tix * 512
            M8 = M8t[tix]

            def scores_kc(kc):
                E("tensor", lambda e: e.matmul(
                    pb[7].h[0:8, 0:L], sgnT.h[:, kc * 8:(kc + 1) * 8],
                    XN[kc].h[:, xb:xb + L],
                    start=(kc == 0), stop=(kc == 7)),
                  reads=[sgnT.iv(kc * 8, kc * 8 + 8), XN[kc].iv(xb, xb + L)],
                  writes=piv(pb[7]))

            for kc in range(NCH):
                scores_kc(kc)
            E("vector", lambda e: e.tensor_copy(scsb.h[0:8, 0:L],
                                                pb[7].h[0:8, 0:L]),
              reads=piv(pb[7]), writes=[scsb.iv()])

            def mask_j(j):
                E("tensor", lambda e: e.transpose(
                    pb[6].h[0:128, 0:8], scsb.h[0:8, j * 128:(j + 1) * 128],
                    idn.h[0:8, 0:8]),
                  reads=[scsb.iv(), idn.iv()], writes=piv(pb[6]))
                E("vector", lambda e: e.tensor_copy(scT.h[0:128, 0:8],
                                                    pb[6].h[0:128, 0:8]),
                  reads=piv(pb[6]), writes=[scT.iv()])
                E("vector", lambda e: e.reduce_max(mxc.h, scT.h[0:128, 0:8],
                                                   axis=AX),
                  reads=[scT.iv()], writes=[mxc.iv()])
                E("vector", lambda e: e.tensor_scalar(
                    mTf.h[0:128, 0:8], scT.h[0:128, 0:8], mxc.h[:], None,
                    op0=OP.is_ge),
                  reads=[scT.iv(), mxc.iv()], writes=[mTf.iv()])
                E("tensor", lambda e: e.transpose(
                    pb[5].h[0:8, 0:128], mTf.h[0:128, 0:8], idn.h[:]),
                  reads=[mTf.iv(), idn.iv()], writes=piv(pb[5]))
                E("vector", lambda e: e.tensor_copy(
                    M8.h[0:8, j * 128:(j + 1) * 128], pb[5].h[0:8, 0:128]),
                  reads=piv(pb[5]), writes=[M8.iv(j * 128, (j + 1) * 128)])

            for j in range(L // 128):
                mask_j(j)

            def pred_e(e2):
                bnk = pb[(e2 * 2 + tix) % 4]
                off = (e2 * 2 + tix) * 512
                E("tensor", lambda e: e.matmul(
                    bnk.h[:, 0:L], selb.h[0:8, e2 * 128:(e2 + 1) * 128],
                    M8.h[0:8, 0:L], start=True, stop=True),
                  reads=[M8.iv(), selb.iv()], writes=piv(bnk))
                E("vector", lambda e: e.tensor_copy(
                    mbu.h[:, off:off + L], bnk.h[:, 0:L]),
                  reads=piv(bnk), writes=[mbu.iv(off, off + L)])

            for e2 in range(8):
                pred_e(e2)

        for tix0, (t0, t1) in enumerate(st["rt"]):
            ffn_scores(tix0, t0, t1)

        def expert(e2):
            ws = wload(wffn[e2])

            def etile(tix):
                xb = fb + tix * 512

                def mm(mc, kc):
                    E("tensor", lambda e: e.matmul(
                        pb[mc].h[:, 0:512],
                        ws.h[:, kc * 1024 + mc * 128:kc * 1024 + (mc + 1) * 128],
                        XN[kc].h[:, xb:xb + 512],
                        start=(kc == 0), stop=(kc == 7)),
                      reads=[ws.iv(kc * 1024, (kc + 1) * 1024),
                             XN[kc].iv(xb, xb + 512)],
                      writes=piv(pb[mc]))

                for mc in range(NCH):
                    for kc in range(NCH):
                        mm(mc, kc)

                def evac(mc):
                    off = (e2 * 2 + tix) * 512
                    E("vector", lambda e: e.copy_predicated(
                        outac[mc].h[:, tix * 512:(tix + 1) * 512],
                        mbu.h[:, off:off + 512], pb[mc].h[:, 0:512]),
                      reads=piv(pb[mc]) + [mbu.iv(off, off + 512),
                                           outac[mc].iv(tix * 512,
                                                        (tix + 1) * 512)],
                      writes=[outac[mc].iv(tix * 512, (tix + 1) * 512)])

                for mc in range(NCH):
                    evac(mc)

            for tix in range(ntile):
                etile(tix)

        for e2 in reversed(range(8)):
            expert(e2)

        def bias_silu(tix, mc):
            t0 = st["rt"][tix][0]
            gcol = t0 - gdom
            s = sil[mc % 2]
            E("tensor", lambda e: e.matmul(
                pb[mc].h[:, 0:512], bffn_sb.h[0:8, mc * 128:(mc + 1) * 128],
                M8t[tix].h[0:8, 0:512], start=True, stop=True),
              reads=[bffn_sb.iv(), M8t[tix].iv()], writes=piv(pb[mc]))
            E("vector", lambda e: e.tensor_tensor(
                s.h[:, 0:512], outac[mc].h[:, tix * 512:(tix + 1) * 512],
                pb[mc].h[:, 0:512], OP.add),
              reads=[outac[mc].iv(tix * 512, (tix + 1) * 512)] + piv(pb[mc]),
              writes=[s.iv()])
            E("scalar", lambda e: e.activation(
                gS[mc].h[:, gcol:gcol + 512], s.h[:, 0:512], AF.Silu),
              reads=[s.iv()], writes=[gS[mc].iv(gcol, gcol + 512)])

        for tix0 in range(ntile):
            for mc in range(NCH):
                bias_silu(tix0, mc)
        if sti == 0:
            tap("g0", gS[0].h[:, 0:512], [gS[0].iv(0, 512)])
            tap("m8", M8t[0].h[0:8, 0:512], [M8t[0].iv(0, 512)])

        mark(f"kvr{sti}")
        arena.off = 0
        mx_ = {q: [arena.take(8 * 256, BF16) for _ in range(2)] for q in "kvr"}
        dsc = arena.take(256, BF16)
        rwt = Buf(dsc.h[:, 0:256], "ARENA", dsc.off, 2, 256)
        ekb2 = [arena.take(256, F32) for _ in range(2)]
        ekvb2 = [arena.take(256, F32) for _ in range(2)]
        numb2 = [arena.take(256, F32) for _ in range(2)]
        exrb2 = [arena.take(256, F32) for _ in range(2)]
        Abuf = arena.take(257, F32)
        Bbuf = arena.take(257, F32)
        wkS = {"k": wload(wdram["Wk"]), "v": wload(wdram["Wv"]),
               "r": wload(wdram["Wr"])}
        tmv = {"k": "tm_k", "v": "tm_v", "r": "tm_r"}
        if sti == 0:
            def mset(q, kc):
                E("gpsimd", lambda e: e.memset(
                    mx_[q][0].h[:, kc * 256:kc * 256 + 1], 0.0),
                  writes=[mx_[q][0].iv(kc * 256, kc * 256 + 1)])
            for q in "kvr":
                for kc in range(NCH):
                    mset(q, kc)

        def kvr_mixes(tix, t0, t1):
            L = t1 - t0
            lc0 = t0 - x0
            first0 = (sti == 0 and tix == 0)
            firstC = (sti == 1 and tix == 0)
            s0 = 1 if (first0 or firstC) else 0
            pp = tix % 2

            def mix_kc(kc):
                cur = xaT[kc].h[:, lc0 + s0:lc0 + L]
                shf = xaT[kc].h[:, lc0 + s0 - 1:lc0 + L - 1]
                riv = [xaT[kc].iv(max(lc0 - 1, 0), lc0 + L)]
                E("gpsimd", lambda e: e.tensor_tensor(
                    dsc.h[:, 0:L - s0], cur, shf, OP.subtract),
                  reads=riv, writes=[dsc.iv()])

                def mix_q(q):
                    E("vector", lambda e: e.scalar_tensor_tensor(
                        mx_[q][pp].h[:, kc * 256 + s0:kc * 256 + L],
                        dsc.h[:, 0:L - s0], vecs[tmv[q]].h[:, kc:kc + 1], shf,
                        op0=OP.mult, op1=OP.add),
                      reads=[dsc.iv(), vecs[tmv[q]].iv()] + riv,
                      writes=[mx_[q][pp].iv(kc * 256 + s0, kc * 256 + L)])

                for q in "kvr":
                    mix_q(q)
                if firstC:
                    E("vector", lambda e: e.tensor_tensor(
                        dsc.h[:, 0:1], xaT[kc].h[:, 0:1],
                        xacar.h[:, kc:kc + 1], OP.subtract),
                      reads=[xaT[kc].iv(0, 1), xacar.iv(kc, kc + 1)],
                      writes=[dsc.iv(0, 2)])

                    def mix0_q(q):
                        E("vector", lambda e: e.scalar_tensor_tensor(
                            mx_[q][pp].h[:, kc * 256:kc * 256 + 1],
                            dsc.h[:, 0:1], vecs[tmv[q]].h[:, kc:kc + 1],
                            xacar.h[:, kc:kc + 1], op0=OP.mult, op1=OP.add),
                          reads=[dsc.iv(0, 2), vecs[tmv[q]].iv(),
                                 xacar.iv(kc, kc + 1)],
                          writes=[mx_[q][pp].iv(kc * 256, kc * 256 + 1)])

                    for q in "kvr":
                        mix0_q(q)

            for kc in range(NCH):
                mix_kc(kc)

        def kvr_tile(tix, t0, t1):
            L = t1 - t0
            first0 = (sti == 0 and tix == 0)
            pp = tix % 2

            def kvr_mc(mc):
                par = mc % 2
                bk, bv, br = pb[par * 3], pb[par * 3 + 1], pb[par * 3 + 2]
                ekb, ekvb = ekb2[par], ekvb2[par]
                numb, exrb = numb2[par], exrb2[par]

                def proj(q, bnk, kc):
                    E("tensor", lambda e: e.matmul(
                        bnk.h[:, 0:L],
                        wkS[q].h[:, kc * 1024 + mc * 128:kc * 1024 + (mc + 1) * 128],
                        mx_[q][pp].h[:, kc * 256:kc * 256 + L],
                        start=(kc == 0), stop=(kc == 7)),
                      reads=[wkS[q].iv(kc * 1024, (kc + 1) * 1024),
                             mx_[q][pp].iv(kc * 256, kc * 256 + L)],
                      writes=piv(bnk))

                for q, bnk in [("k", bk), ("v", bv), ("r", br)]:
                    for kc in range(NCH):
                        proj(q, bnk, kc)
                if first0:
                    E("vector", lambda e: e.scalar_tensor_tensor(
                        ekb.h[:, 0:L], bk.h[:, 0:L], 1.0, lmB.h[:, 0:L],
                        op0=OP.mult, op1=OP.add),
                      reads=piv(bk) + [lmB.iv()], writes=[ekb.iv()])
                    E("scalar", lambda e: e.activation(ekb.h[:, 0:L],
                                                       ekb.h[:, 0:L], AF.Exp),
                      reads=[ekb.iv()], writes=[ekb.iv()])
                else:
                    E("scalar", lambda e: e.activation(ekb.h[:, 0:L],
                                                       bk.h[:, 0:L], AF.Exp),
                      reads=piv(bk), writes=[ekb.iv()])
                E("scalar", lambda e: e.activation(numb.h[:, 0:L],
                                                   bv.h[:, 0:L], AF.Copy),
                  reads=piv(bv), writes=[numb.iv()])
                E("scalar", lambda e: e.activation(exrb.h[:, 0:L], br.h[:, 0:L],
                                                   AF.Sigmoid),
                  reads=piv(br), writes=[exrb.iv()])
                E("vector", lambda e: e.tensor_tensor(
                    ekvb.h[:, 0:L], numb.h[:, 0:L], ekb.h[:, 0:L], OP.mult),
                  reads=[numb.iv(), ekb.iv()], writes=[ekvb.iv()])

                def scan(buf, data, stv):
                    E("vector", lambda e: e.tensor_copy(buf.h[:, 0:1],
                                                        stv.h[:, mc:mc + 1]),
                      reads=[stv.iv(mc, mc + 1)], writes=[buf.iv(0, 1)])
                    E("vector", lambda e: e.tensor_tensor_scan(
                        buf.h[:, 1:L + 1],
                        vecs["ew"].h[:, mc:mc + 1].broadcast_to([128, L]),
                        data.h[:, 0:L], initial=stv.h[:, mc:mc + 1],
                        op0=OP.mult, op1=OP.add),
                      reads=[data.iv(), stv.iv(mc, mc + 1), vecs["ew"].iv()],
                      writes=[buf.iv(1, L + 1)])
                    E("vector", lambda e: e.tensor_copy(stv.h[:, mc:mc + 1],
                                                        buf.h[:, L:L + 1]),
                      reads=[buf.iv(L, L + 1)], writes=[stv.iv(mc, mc + 1)])

                scan(Abuf, ekvb, Ast)
                scan(Bbuf, ekb, Bst)
                E("vector", lambda e: e.scalar_tensor_tensor(
                    numb.h[:, 0:L], ekvb.h[:, 0:L],
                    vecs["eu"].h[:, mc:mc + 1], Abuf.h[:, 0:L],
                    op0=OP.mult, op1=OP.add),
                  reads=[ekvb.iv(), vecs["eu"].iv(), Abuf.iv(0, L)],
                  writes=[numb.iv()])
                E("vector", lambda e: e.scalar_tensor_tensor(
                    ekb.h[:, 0:L], ekb.h[:, 0:L],
                    vecs["eu"].h[:, mc:mc + 1], Bbuf.h[:, 0:L],
                    op0=OP.mult, op1=OP.add),
                  reads=[ekb.iv(), vecs["eu"].iv(), Bbuf.iv(0, L)],
                  writes=[ekb.iv()])
                E("scalar", lambda e: e.activation(ekvb.h[:, 0:L],
                                                   ekb.h[:, 0:L], AF.Ln,
                                                   bias=tinycol.h[:]),
                  reads=[ekb.iv(), tinycol.iv()], writes=[ekvb.iv()])
                E("scalar", lambda e: e.activation(ekvb.h[:, 0:L],
                                                   ekvb.h[:, 0:L], AF.Exp,
                                                   scale=-1.0),
                  reads=[ekvb.iv()], writes=[ekvb.iv()])
                E("vector", lambda e: e.tensor_tensor(
                    numb.h[:, 0:L], numb.h[:, 0:L], ekvb.h[:, 0:L], OP.mult),
                  reads=[numb.iv(), ekvb.iv()], writes=[numb.iv()])
                ds = max(t0, dom) - dom
                so = max(t0, dom) - t0
                n = t1 - max(t0, dom)
                if first0:
                    E("vector", lambda e: e.tensor_tensor(
                        rwt.h, numb.h[:, 0:L], exrb.h[:, 0:L], OP.mult),
                      reads=[numb.iv(), exrb.iv()], writes=[rwt.iv()])
                    E("vector", lambda e: e.tensor_copy(
                        rwk[mc].h[:, ds:ds + n], zerot.h[:, 0:n]),
                      reads=[zerot.iv()], writes=[rwk[mc].iv(ds, ds + n)])
                    E("vector", lambda e: e.copy_predicated(
                        rwk[mc].h[:, ds:ds + n], m01u.h[:, so:so + n],
                        rwt.h[:, so:so + n]),
                      reads=[rwt.iv(), m01u.iv(), rwk[mc].iv(ds, ds + n)],
                      writes=[rwk[mc].iv(ds, ds + n)])
                else:
                    E("gpsimd", lambda e: e.tensor_tensor(
                        rwk[mc].h[:, ds:ds + n], numb.h[:, 0:n],
                        exrb.h[:, 0:n], OP.mult),
                      reads=[numb.iv(), exrb.iv()],
                      writes=[rwk[mc].iv(ds, ds + n)])

            for mc in range(NCH):
                kvr_mc(mc)

        ktiles = st["kvr"]
        kvr_mixes(0, *ktiles[0])
        for tix, (t0, t1) in enumerate(ktiles):
            if tix + 1 < len(ktiles):
                kvr_mixes(tix + 1, *ktiles[tix + 1])
            kvr_tile(tix, t0, t1)
        if sti == 0:
            tap("rwk0", rwk[0].h[:, 0:512], [rwk[0].iv(0, 512)])

        mark(f"wo{sti}")
        arena.off = 0
        sqx = [arena.take(512, F32) for _ in range(2)]
        t1s = arena.take(512, F32)
        t1g = arena.take(512, F32)
        Rb = arena.take(512, F32)
        M2b = arena.take(512, F32)
        rows = {nm: arena.take(512, F32) for nm in ["ssq", "sx", "rstd", "m2"]}
        kkq = arena.take(16 * 512, BF16)
        kvacc = [Buf(rwk[mc].h[:, 0:1024], f"rwk{mc}", 0, 2, 1024)
                 for mc in range(NCH)]
        mxck = arena.take(8 * 1024, BF16)
        rls = [arena.take(512, BF16) for _ in range(2)]
        kvf = arena.take(512, F32)
        dsc3 = arena.take(512, BF16)
        mxcr = Buf(ar16[:, (kkq.off + 8192) // 2:(kkq.off + 8192) // 2 + 4096],
                   "ARENA", kkq.off + 8192, 2, 4096)
        dsc2 = arena.take(512, BF16)
        ntile = len(st["rt"])
        woS = wload(wdram["Wo"])
        wcrS = wload(wdram["Wcr"])
        ckw = {0: [wload(wdram["Wck"][0]), None]}

        def wo_mm_tile(g0, g1):
            L = g1 - g0
            bcol = g0 - dom

            def wo_mc(mc):
                bnk = pb[6 + mc % 2]

                def mm(kc):
                    E("tensor", lambda e: e.matmul(
                        bnk.h[:, 0:L],
                        woS.h[:, kc * 1024 + mc * 128:kc * 1024 + (mc + 1) * 128],
                        rwk[kc].h[:, bcol:bcol + L],
                        start=(kc == 0), stop=(kc == 7)),
                      reads=[woS.iv(kc * 1024, (kc + 1) * 1024),
                             rwk[kc].iv(bcol, bcol + L)],
                      writes=piv(bnk))

                for kc in range(NCH):
                    mm(kc)
                E("vector", lambda e: e.tensor_tensor(
                    XN[mc].h[:, bcol:bcol + L], bnk.h[:, 0:L],
                    XN[mc].h[:, bcol:bcol + L], OP.add),
                  reads=piv(bnk) + [XN[mc].iv(bcol, bcol + L)],
                  writes=[XN[mc].iv(bcol, bcol + L)])

            for mc in range(NCH):
                wo_mc(mc)

        def wo_tile(gix, g0, g1):
            L = g1 - g0
            bcol = g0 - dom

            def stats_kc(kc):
                sq = sqx[kc % 2]
                E("scalar", lambda e: e.activation(
                    sq.h[:, 0:L], XN[kc].h[:, bcol:bcol + L], AF.Square),
                  reads=[XN[kc].iv(bcol, bcol + L)], writes=[sq.iv()])
                E("tensor", lambda e: e.matmul(
                    pb[2].h[0:1, 0:L], onescol.h[:], sq.h[:, 0:L],
                    start=(kc == 0), stop=(kc == 7)),
                  reads=[sq.iv(), onescol.iv()], writes=piv(pb[2]))
                E("tensor", lambda e: e.matmul(
                    pb[3].h[0:1, 0:L], onescol_b.h[:],
                    XN[kc].h[:, bcol:bcol + L],
                    start=(kc == 0), stop=(kc == 7)),
                  reads=[XN[kc].iv(bcol, bcol + L), onescol_b.iv()],
                  writes=piv(pb[3]))

            for kc in range(NCH):
                stats_kc(kc)
            E("vector", lambda e: e.tensor_copy(rows["ssq"].h[0:1, 0:L],
                                                pb[2].h[0:1, 0:L]),
              reads=piv(pb[2]), writes=[rows["ssq"].iv()])
            E("vector", lambda e: e.tensor_copy(rows["sx"].h[0:1, 0:L],
                                                pb[3].h[0:1, 0:L]),
              reads=piv(pb[3]), writes=[rows["sx"].iv()])
            E("vector", lambda e: e.tensor_scalar(
                rows["sx"].h[0:1, 0:L], rows["sx"].h[0:1, 0:L], 1.0 / C, None,
                op0=OP.mult),
              reads=[rows["sx"].iv()], writes=[rows["sx"].iv()])
            E("vector", lambda e: e.tensor_tensor(
                rows["m2"].h[0:1, 0:L], rows["sx"].h[0:1, 0:L],
                rows["sx"].h[0:1, 0:L], OP.mult),
              reads=[rows["sx"].iv()], writes=[rows["m2"].iv()])
            E("vector", lambda e: e.scalar_tensor_tensor(
                rows["ssq"].h[0:1, 0:L], rows["ssq"].h[0:1, 0:L], 1.0 / C,
                rows["m2"].h[0:1, 0:L], op0=OP.mult, op1=OP.subtract),
              reads=[rows["ssq"].iv(), rows["m2"].iv()],
              writes=[rows["ssq"].iv()])
            E("scalar", lambda e: e.activation(
                rows["ssq"].h[0:1, 0:L], rows["ssq"].h[0:1, 0:L], AF.Ln,
                bias=epscol.h[0:1, 0:1]),
              reads=[rows["ssq"].iv(), epscol.iv()],
              writes=[rows["ssq"].iv()])
            E("scalar", lambda e: e.activation(
                rows["rstd"].h[0:1, 0:L], rows["ssq"].h[0:1, 0:L], AF.Exp,
                scale=-0.5),
              reads=[rows["ssq"].iv()], writes=[rows["rstd"].iv()])
            E("vector", lambda e: e.tensor_tensor(
                rows["m2"].h[0:1, 0:L], rows["sx"].h[0:1, 0:L],
                rows["rstd"].h[0:1, 0:L], OP.mult),
              reads=[rows["sx"].iv(), rows["rstd"].iv()],
              writes=[rows["m2"].iv()])
            E("tensor", lambda e: e.matmul(
                pb[4].h[:, 0:L], ones_f.h[:], rows["rstd"].h[0:1, 0:L],
                start=True, stop=True),
              reads=[rows["rstd"].iv(), ones_f.iv()], writes=piv(pb[4]))
            E("vector", lambda e: e.tensor_copy(Rb.h[:, 0:L], pb[4].h[:, 0:L]),
              reads=piv(pb[4]), writes=[Rb.iv()])
            E("tensor", lambda e: e.matmul(
                pb[5].h[:, 0:L], ones_f.h[:], rows["m2"].h[0:1, 0:L],
                start=True, stop=True),
              reads=[rows["m2"].iv(), ones_f.iv()], writes=piv(pb[5]))
            E("vector", lambda e: e.tensor_copy(M2b.h[:, 0:L], pb[5].h[:, 0:L]),
              reads=piv(pb[5]), writes=[M2b.iv()])

            def xc_kc(kc):
                eng = "vector"
                ts = t1s if kc % 2 else t1g
                E(eng, lambda e: e.tensor_tensor(
                    ts.h[:, 0:L], XN[kc].h[:, bcol:bcol + L], Rb.h[:, 0:L],
                    OP.mult),
                  reads=[XN[kc].iv(bcol, bcol + L), Rb.iv()], writes=[ts.iv()])
                E(eng, lambda e: e.tensor_tensor(
                    ts.h[:, 0:L], ts.h[:, 0:L], M2b.h[:, 0:L], OP.subtract),
                  reads=[ts.iv(), M2b.iv()], writes=[ts.iv()])
                E(eng, lambda e: e.tensor_scalar(
                    xcS[kc].h[:, bcol:bcol + L], ts.h[:, 0:L],
                    vecs["ln1_g"].h[:, kc:kc + 1],
                    vecs["ln1_b"].h[:, kc:kc + 1], op0=OP.mult, op1=OP.add),
                  reads=[ts.iv(), vecs["ln1_g"].iv(), vecs["ln1_b"].iv()],
                  writes=[xcS[kc].iv(bcol, bcol + L)])
                if sti == 0 and gix == 0:
                    E("vector", lambda e: e.tensor_tensor(
                        xcS[kc].h[:, 0:128], xcS[kc].h[:, 0:128],
                        m01b.h[:, 128:256], OP.mult),
                      reads=[xcS[kc].iv(0, 128), m01b.iv()],
                      writes=[xcS[kc].iv(0, 128)])

            for kc in range(NCH):
                xc_kc(kc)

        def cmix(dst, base, kc, bcol, firstC, cmvec, dscb):
            s0 = 1 if firstC else 0
            cur = xcS[kc].h[:, bcol + s0:bcol + 512]
            shf = xcS[kc].h[:, bcol + s0 - 1:bcol + 511]
            riv = [xcS[kc].iv(max(bcol - 1, 0), bcol + 512)]
            E("vector", lambda e: e.tensor_tensor(
                dscb.h[:, 0:512 - s0], cur, shf, OP.subtract),
              reads=riv, writes=[dscb.iv()])
            E("vector", lambda e: e.scalar_tensor_tensor(
                dst.h[:, base + s0:base + 512], dscb.h[:, 0:512 - s0],
                vecs[cmvec].h[:, kc:kc + 1], shf, op0=OP.mult, op1=OP.add),
              reads=[dscb.iv(), vecs[cmvec].iv()] + riv,
              writes=[dst.iv(base + s0, base + 512)])
            if firstC:
                E("vector", lambda e: e.tensor_tensor(
                    dscb.h[:, 0:1], xcS[kc].h[:, 0:1], xccar.h[:, kc:kc + 1],
                    OP.subtract),
                  reads=[xcS[kc].iv(0, 1), xccar.iv(kc, kc + 1)],
                  writes=[dscb.iv(0, 2)])
                E("vector", lambda e: e.scalar_tensor_tensor(
                    dst.h[:, base:base + 1], dscb.h[:, 0:1],
                    vecs[cmvec].h[:, kc:kc + 1], xccar.h[:, kc:kc + 1],
                    op0=OP.mult, op1=OP.add),
                  reads=[dscb.iv(0, 2), vecs[cmvec].iv(), xccar.iv(kc, kc + 1)],
                  writes=[dst.iv(base, base + 1)])

        def wcr_tile(tix, t0):
            bcol = t0 - dom
            firstC = (sti == 1 and tix == 0)
            for kc in range(NCH):
                cmix(mxcr, kc * 512, kc, bcol, firstC, "cm_r", dsc2)

            def wcr_mc(mc):
                bnk = pb[mc % 2]

                def mm(kc):
                    E("tensor", lambda e: e.matmul(
                        bnk.h[:, 0:512],
                        wcrS.h[:, kc * 1024 + mc * 128:kc * 1024 + (mc + 1) * 128],
                        mxcr.h[:, kc * 512:(kc + 1) * 512],
                        start=(kc == 0), stop=(kc == 7)),
                      reads=[wcrS.iv(kc * 1024, (kc + 1) * 1024),
                             mxcr.iv(kc * 512, (kc + 1) * 512)],
                      writes=piv(bnk))

                for kc in range(NCH):
                    mm(kc)
                E("scalar", lambda e: e.activation(
                    xaT[mc].h[:, tix * 512:(tix + 1) * 512], bnk.h[:, 0:512],
                    AF.Sigmoid),
                  reads=piv(bnk),
                  writes=[xaT[mc].iv(tix * 512, (tix + 1) * 512)])

            for mc in range(NCH):
                wcr_mc(mc)

        def ck_kk(q, tix):
            wslA = ckw[q][0]

            def kk_fl(fl):
                bnk = pb[fl % 2]

                def mm(kc):
                    E("tensor", lambda e: e.matmul(
                        bnk.h[:, 0:512],
                        wslA.h[:, kc * 1024 + fl * 128:kc * 1024 + (fl + 1) * 128],
                        mxck.h[:, kc * 1024 + tix * 512:kc * 1024 + (tix + 1) * 512],
                        start=(kc == 0), stop=(kc == 7)),
                      reads=[wslA.iv(kc * 1024, (kc + 1) * 1024),
                             mxck.iv(kc * 1024 + tix * 512,
                                     kc * 1024 + (tix + 1) * 512)],
                      writes=piv(bnk))

                for kc in range(NCH):
                    mm(kc)
                rl = rls[fl % 2]
                E("scalar", lambda e: e.activation(
                    rl.h[:, 0:512], bnk.h[:, 0:512], AF.Relu),
                  reads=piv(bnk), writes=[rl.iv()])
                kq = (tix * 8 + fl) * 512
                E("gpsimd", lambda e: e.tensor_tensor(
                    kkq.h[:, kq:kq + 512], rl.h[:, 0:512], rl.h[:, 0:512],
                    OP.mult),
                  reads=[rl.iv()], writes=[kkq.iv(kq, kq + 512)])

            for fl in range(8):
                kk_fl(fl)

        def ck_kv(q, tix):
            wslB = ckw[q][1]

            def kv_mc(mc):
                bnk = pb[2 + mc % 4]

                def mm(fl):
                    kq = (tix * 8 + fl) * 512
                    E("tensor", lambda e: e.matmul(
                        bnk.h[:, 0:512],
                        wslB.h[:, fl * 1024 + mc * 128:fl * 1024 + (mc + 1) * 128],
                        kkq.h[:, kq:kq + 512],
                        start=(fl == 0), stop=(fl == 7)),
                      reads=[wslB.iv(fl * 1024, (fl + 1) * 1024),
                             kkq.iv(kq, kq + 512)],
                      writes=piv(bnk))

                for fl in range(8):
                    mm(fl)
                acc = kvacc[mc]
                if q == 0:
                    E("vector", lambda e: e.tensor_copy(
                        acc.h[:, tix * 512:(tix + 1) * 512], bnk.h[:, 0:512]),
                      reads=piv(bnk), writes=[acc.iv(tix * 512, (tix + 1) * 512)])
                else:
                    E("vector", lambda e: e.tensor_tensor(
                        acc.h[:, tix * 512:(tix + 1) * 512], bnk.h[:, 0:512],
                        acc.h[:, tix * 512:(tix + 1) * 512], OP.add),
                      reads=piv(bnk) + [acc.iv(tix * 512, (tix + 1) * 512)],
                      writes=[acc.iv(tix * 512, (tix + 1) * 512)])

            for mc in range(NCH):
                kv_mc(mc)

        for g0_, g1_ in st["gt"]:
            wo_mm_tile(g0_, g1_)
        rt_emitted = [False] * ntile
        for gix, (g0, g1) in enumerate(st["gt"]):
            wo_tile(gix, g0, g1)
            for t_, (rt0_, rt1_) in enumerate(st["rt"]):
                if not rt_emitted[t_] and rt1_ <= g1:
                    rt_emitted[t_] = True
                    firstC_ = (sti == 1 and t_ == 0)
                    wcr_tile(t_, rt0_)
                    for kc in range(NCH):
                        cmix(mxck, kc * 1024 + t_ * 512, kc, rt0_ - dom,
                             firstC_, "cm_k", dsc3)
                    ck_kk(0, t_)
        if sti == 0:
            tap("x1", XN[0].h[:, 0:512], [XN[0].iv(0, 512)])
            tap("xc0", xcS[0].h[:, 0:512], [xcS[0].iv(0, 512)])
            tap("rr0", xaT[0].h[:, 0:512], [xaT[0].iv(0, 512)])

            def xc_car(kc):
                E("vector", lambda e: e.tensor_copy(
                    xccar.h[:, kc:kc + 1], xcS[kc].h[:, 1151:1152]),
                  reads=[xcS[kc].iv(1151, 1152)], writes=[xccar.iv(kc, kc + 1)])
            for kc in range(NCH):
                xc_car(kc)

        ckw[0][1] = wload(wdram["Wcv"][0])
        for tix0 in range(ntile):
            ck_kv(0, tix0)
        for q in range(1, 4):
            ckw[q] = [wload(wdram["Wck"][q]), wload(wdram["Wcv"][q])]
            for tix0 in range(ntile):
                ck_kk(q, tix0)
            for tix0 in range(ntile):
                ck_kv(q, tix0)

        def s_tile(tix, t0):
            bcol = t0 - dom

            def s_mc(mc):
                E("vector", lambda e: e.tensor_tensor(
                    kvf.h[:, 0:512], kvacc[mc].h[:, tix * 512:(tix + 1) * 512],
                    xaT[mc].h[:, tix * 512:(tix + 1) * 512], OP.mult),
                  reads=[kvacc[mc].iv(tix * 512, (tix + 1) * 512),
                         xaT[mc].iv(tix * 512, (tix + 1) * 512)],
                  writes=[kvf.iv()])
                E("vector", lambda e: e.tensor_tensor(
                    XN[mc].h[:, bcol:bcol + 512], XN[mc].h[:, bcol:bcol + 512],
                    kvf.h[:, 0:512], OP.add),
                  reads=[XN[mc].iv(bcol, bcol + 512), kvf.iv()],
                  writes=[XN[mc].iv(bcol, bcol + 512)])

            for mc in range(NCH):
                s_mc(mc)

        for tix, (t0, t1) in enumerate(st["rt"]):
            s_tile(tix, t0)

        arena.off = 0
        sgF = [arena.take(512, F32) for _ in range(NCH)]
        xrw = [arena.take(1024, F32) for _ in range(6)]
        osb = [arena.take(1024, F32) for _ in range(2)]

        def out_tile(t0, tb):
            bcol = t0 - dom
            gcol = t0 - gdom

            def sg_mc(mc):
                E("vector", lambda e: e.tensor_tensor(
                    sgF[mc].h[:, 0:512], XN[mc].h[:, bcol:bcol + 512],
                    gS[mc].h[:, gcol:gcol + 512], OP.mult),
                  reads=[XN[mc].iv(bcol, bcol + 512),
                         gS[mc].iv(gcol, gcol + 512)],
                  writes=[sgF[mc].iv()])

            for mc in range(NCH):
                sg_mc(mc)

            def load_j(j):
                r0 = t0 + j * 128
                s = (tb + j) % 6
                E("sync", lambda e: e.dma_start(out=xrw[s].h[:],
                                                in_=xsl[r0:r0 + 128, :]),
                  writes=[xrw[s].iv()], dma=f"xr{s}")

            for j in range(4):
                load_j(j)

            def out_j(j):
                r0 = t0 + j * 128
                par = j % 2

                def tr_mc(mc):
                    bnk = pb[mc]
                    E("tensor", lambda e: e.transpose(
                        bnk.h[:, 0:128], sgF[mc].h[:, j * 128:(j + 1) * 128],
                        idn.h[:]),
                      reads=[sgF[mc].iv(j * 128, (j + 1) * 128), idn.iv()],
                      writes=piv(bnk))
                    E("vector", lambda e: e.tensor_tensor(
                        osb[par].h[:, mc * 128:(mc + 1) * 128], bnk.h[:, 0:128],
                        xrw[(tb + j) % 6].h[:, mc * 128:(mc + 1) * 128], OP.add),
                      reads=piv(bnk) + [xrw[(tb + j) % 6].iv()],
                      writes=[osb[par].iv(mc * 128, (mc + 1) * 128)])

                for mc in range(NCH):
                    tr_mc(mc)
                E("sync", lambda e: e.dma_start(
                    out=osl[r0 - HALO:r0 - HALO + 128, :], in_=osb[par].h[:]),
                  reads=[osb[par].iv()], writes=[], dma=f"ost{par}")

            for j in range(4):
                out_j(j)

        for tix0, (t0, t1) in enumerate(st["rt"]):
            out_tile(t0, tix0 * 4)

    for sti_, st_ in enumerate(STS):
        do_st(sti_, st_)

    dkeys = P.finalize()
    guaranteed = {e: {} for e in Prog.ENGS}
    with contextlib.ExitStack() as ctx:
        esems = {e: ctx.enter_context(nc.semaphore("s_" + e)) for e in Prog.ENGS}
        dsems = {k: ctx.enter_context(nc.semaphore("d_" + k)) for k in dkeys}
        blk = ctx.enter_context(nc.Block())

        @blk.tensor
        def _(t):
            P.replay("tensor", t, esems, dsems, guaranteed)

        @blk.vector
        def _(v):
            P.replay("vector", v, esems, dsems, guaranteed)

        @blk.scalar
        def _(s):
            P.replay("scalar", s, esems, dsems, guaranteed)

        @blk.gpsimd
        def _(g):
            P.replay("gpsimd", g, esems, dsems, guaranteed)

        @blk.sync
        def _(sy):
            P.replay("sync", sy, esems, dsems, guaranteed)
            for k in dkeys:
                sy.wait_ge(dsems[k], P.dtot[k] * 16)
    return nc


def _host_maps(inputs):
    x = np.ascontiguousarray(np.asarray(inputs["L0_attn_x"], dtype=np.float32))

    def f32(a):
        return np.asarray(a, dtype=np.float32)

    def sq(w):
        return np.ascontiguousarray(
            f32(w).reshape(8, 128, 1024).transpose(1, 0, 2).reshape(128, 8192))

    shared = {
        "Wk": sq(inputs["L0_attn_Wk"]), "Wv": sq(inputs["L0_attn_Wv"]),
        "Wr": sq(inputs["L0_attn_Wr"]), "Wo": sq(inputs["L0_attn_Wo"]),
        "Wcr": sq(inputs["L0_attn_Wcr"]),
        "Wck": np.ascontiguousarray(f32(inputs["L0_attn_Wck"]).reshape(
            8, 128, 4, 1024).transpose(2, 1, 0, 3).reshape(4, 128, 8192)),
        "Wcv": np.ascontiguousarray(f32(inputs["L0_attn_Wcv"]).reshape(
            4, 8, 128, 1024).transpose(0, 2, 1, 3).reshape(4, 128, 8192)),
        "Wffn": np.ascontiguousarray(f32(inputs["L0_ffn_W"]).reshape(
            8, 8, 128, 1024).transpose(0, 2, 1, 3).reshape(8, 128, 8192)),
        "bffn": inputs["L0_ffn_b"], "sigs": inputs["L0_ffn_tile_sigs"],
        "ln0_g": inputs["L0_attn_ln0_g"], "ln0_b": inputs["L0_attn_ln0_b"],
        "ln1_g": inputs["L0_attn_ln1_g"], "ln1_b": inputs["L0_attn_ln1_b"],
        "time_decay": inputs["L0_attn_time_decay"],
        "time_first": inputs["L0_attn_time_first"],
        "tm_k": inputs["L0_attn_tm_k"], "tm_v": inputs["L0_attn_tm_v"],
        "tm_r": inputs["L0_attn_tm_r"], "cm_k": inputs["L0_attn_cm_k"],
        "cm_r": inputs["L0_attn_cm_r"],
    }
    shared = {k: np.ascontiguousarray(np.asarray(v, dtype=np.float32))
              for k, v in shared.items()}
    in_maps = []
    for core in range(8):
        b = core // 4
        t0 = (core % 4) * REAL
        lo = t0 - HALO
        xs = np.zeros((EXT, C), np.float32)
        msk = np.zeros((1, HALO), np.float32)
        src_lo = max(lo, 0)
        xs[src_lo - lo:] = x[b, src_lo:t0 + REAL]
        msk[0, src_lo - lo:] = 1.0
        msk[0, 0] = 0.0
        lr = ((1.0 - msk) * LOGMASK).astype(np.float32)
        im = dict(shared)
        im["xsl"] = xs
        im["mrow"] = msk
        im["lrow"] = lr
        in_maps.append(im)
    return in_maps


def kernel(**inputs):
    nc = build_nc()
    in_maps = _host_maps(inputs)
    res = run_bass_kernel_spmd(nc, in_maps, core_ids=list(range(8)))
    out = np.zeros((2, 8192, C), np.float32)
    for core in range(8):
        b = core // 4
        t0 = (core % 4) * REAL
        out[b, t0:t0 + REAL] = res.results[core]["osl"]
    return out


def run_debug(inputs, taps):
    global TAPS_WANT
    TAPS_WANT = tuple(taps)
    try:
        nc = build_nc()
        in_maps = _host_maps(inputs)
        res = run_bass_kernel_spmd(nc, in_maps, core_ids=list(range(8)))
        out = np.zeros((2, 8192, C), np.float32)
        for core in range(8):
            b = core // 4
            t0 = (core % 4) * REAL
            out[b, t0:t0 + REAL] = res.results[core]["osl"]
        dbgs = [res.results[core].get("dbg") for core in range(8)]
        return out, dbgs
    finally:
        TAPS_WANT = ()
SERIAL = False
SERIAL_FROM = 0
SERIAL_TO = 10**9
ONLY_LOAD = False
PHASE_MARKS = {}


if __name__ == "__main__":
    import sys
    if "--build" in sys.argv:
        nc_ = build_nc()
        print("built ok; instructions:", nc_.n_instructions())


# version 23
# speedup vs baseline: 1.0943x; 1.0943x over previous
import contextlib
import numpy as np
import concourse.bass as bass
import concourse.mybir as mybir
from concourse.bass_utils import run_bass_kernel_spmd

F32 = mybir.dt.float32
BF16 = mybir.dt.bfloat16
U8 = mybir.dt.uint8
AF = mybir.ActivationFunctionType
OP = mybir.AluOpType
AX = mybir.AxisListType.X

TAPS_WANT = ()
SERIAL = False
SERIAL_FROM = 0
SERIAL_TO = 10**9
ONLY_LOAD = False
PHASE_MARKS = {}

C = 1024
NCH = 8
HALO = 256
REAL = 2048
EXT = HALO + REAL
EPS = 1e-5
LOGMASK = -30000.0

STS = [
    dict(x0=0, x1=1280, dom=128, gdom=256,
         kvr=[(0, 256), (256, 768), (768, 1280)],
         gt=[(128, 256), (256, 768), (768, 1280)],
         rt=[(256, 768), (768, 1280)]),
    dict(x0=1280, x1=2304, dom=1280, gdom=1280,
         kvr=[(1280, 1792), (1792, 2304)],
         gt=[(1280, 1792), (1792, 2304)],
         rt=[(1280, 1792), (1792, 2304)]),
]


class Buf:
    def __init__(self, handle, track, byte_off, ebytes, cols):
        self.h = handle
        self.track = track
        self.off = byte_off
        self.eb = ebytes
        self.cols = cols

    def iv(self, lo=0, hi=None):
        if hi is None:
            hi = self.cols
        return (self.track, self.off + lo * self.eb, self.off + hi * self.eb)


class Prog:
    ENGS = ("tensor", "vector", "scalar", "gpsimd", "sync")

    def __init__(self):
        self.ops = []
        self.hist = {}

    def emit(self, eng, fn, reads=(), writes=(), dma=None, tag=""):
        i = len(self.ops)
        deps = set()
        for (t, lo, hi) in reads:
            h = self.hist.get(t)
            if h:
                for (l2, h2, j) in h["w"]:
                    if l2 < hi and lo < h2:
                        deps.add(j)
        for (t, lo, hi) in writes:
            h = self.hist.get(t)
            if h:
                lw = -1
                for (l2, h2, j) in h["w"]:
                    if l2 < hi and lo < h2:
                        deps.add(j)
                        lw = max(lw, j)
                for (l2, h2, j) in h["r"]:
                    if l2 < hi and lo < h2 and j > lw:
                        deps.add(j)
        self.ops.append({"eng": eng, "fn": fn, "deps": deps, "dma": dma,
                         "tag": tag, "need_inc": dma is not None})
        for (t, lo, hi) in reads:
            self.hist.setdefault(t, {"w": [], "r": []})["r"].append((lo, hi, i))
        for (t, lo, hi) in writes:
            self.hist.setdefault(t, {"w": [], "r": []})["w"].append((lo, hi, i))
        return i

    def finalize(self):
        ops = self.ops
        if SERIAL:
            for i, op in enumerate(ops):
                if SERIAL_FROM < i < SERIAL_TO:
                    op["deps"].add(i - 1)
        for op in ops:
            for j in op["deps"]:
                p = ops[j]
                if p["eng"] == op["eng"] == "tensor" and p["dma"] is None:
                    continue
                p["need_inc"] = True
        ecnt = {e: 0 for e in self.ENGS}
        dcnt = {}
        for op in ops:
            if op["dma"] is not None:
                k = op["dma"]
                dcnt[k] = dcnt.get(k, 0) + 1
                op["inc_val"] = dcnt[k] * 16
            elif op["need_inc"]:
                ecnt[op["eng"]] += 1
                op["inc_val"] = ecnt[op["eng"]]
        pref = {}
        run = {}
        for i, op in enumerate(ops):
            pref[i] = None
            if op["dma"] is not None:
                run[op["dma"]] = run.get(op["dma"], 0) + 1
                op["kcount"] = run[op["dma"]]
        for i, op in enumerate(ops):
            need = {}
            for j in op["deps"]:
                p = ops[j]
                if p["dma"] is not None:
                    key = ("d", p["dma"])
                    v = 16 * self._emitted_before(ops, p["dma"], i)
                elif p["eng"] == op["eng"] == "tensor":
                    continue
                else:
                    key = ("e", p["eng"])
                    v = p["inc_val"]
                if need.get(key, 0) < v:
                    need[key] = v
            op["waits"] = need
        self.dtot = dcnt
        return sorted(dcnt)

    def _emitted_before(self, ops, key, idx):
        cache = getattr(self, "_ebcache", None)
        if cache is None or cache[0] is not ops:
            pc = {}
            counts = {}
            for i, op in enumerate(ops):
                pc[i] = dict(counts) if op["deps"] else None
                if op["dma"] is not None:
                    counts[op["dma"]] = counts.get(op["dma"], 0) + 1
            self._ebcache = (ops, pc, counts)
        pc = self._ebcache[1][idx]
        if pc is None:
            pc = {}
        return pc.get(key, 0)

    def replay(self, eng_name, eng_obj, esems, dsems, guaranteed):
        for op in self.ops:
            if op["eng"] != eng_name:
                continue
            g = guaranteed[eng_name]
            for (kind, key), v in sorted(op["waits"].items()):
                if g.get((kind, key), 0) >= v:
                    continue
                eng_obj.wait_ge(dsems[key] if kind == "d" else esems[key], v)
                g[(kind, key)] = v
            inst = op["fn"](eng_obj)
            if op["dma"] is not None:
                inst.then_inc(dsems[op["dma"]], 16)
            elif op["need_inc"]:
                inst.then_inc(esems[eng_name], 1)


def build_nc():
    nc = bass.Bass("TRN2", target_bir_lowering=False)

    xsl = nc.dram_tensor("xsl", [EXT, C], F32, kind="ExternalInput").ap()
    mrow = nc.dram_tensor("mrow", [1, HALO], F32, kind="ExternalInput").ap()
    lrow = nc.dram_tensor("lrow", [1, HALO], F32, kind="ExternalInput").ap()
    wdram = {}
    for nm in ["Wk", "Wv", "Wr", "Wo", "Wcr"]:
        wdram[nm] = nc.dram_tensor(nm, [128, 8192], F32,
                                   kind="ExternalInput").ap()
    for nm in ["Wck", "Wcv"]:
        wdram[nm] = nc.dram_tensor(nm, [4, 128, 8192], F32,
                                   kind="ExternalInput").ap()
    wffn = nc.dram_tensor("Wffn", [8, 128, 8192], F32,
                          kind="ExternalInput").ap()
    bffn = nc.dram_tensor("bffn", [8, C], F32, kind="ExternalInput").ap()
    sigs = nc.dram_tensor("sigs", [8, C], F32, kind="ExternalInput").ap()
    vec_in = {}
    for nm in ["ln0_g", "ln0_b", "ln1_g", "ln1_b", "time_decay", "time_first",
               "tm_k", "tm_v", "tm_r", "cm_k", "cm_r"]:
        vec_in[nm] = nc.dram_tensor(nm, [C], F32, kind="ExternalInput").ap()
    osl = nc.dram_tensor("osl", [REAL, C], F32, kind="ExternalOutput").ap()

    def alloc(name, shape, dt):
        h = nc.alloc_sbuf_tensor(name, shape, dt)
        eb = {BF16: 2, U8: 1}.get(dt, 4)
        return Buf(h, name, 0, eb, shape[-1])

    xaT = [alloc(f"xaT{k}", [128, 1280], BF16) for k in range(NCH)]
    XN = [alloc(f"XN{k}", [128, 1152], BF16) for k in range(NCH)]
    rwk = [alloc(f"rwk{k}", [128, 1152], BF16) for k in range(NCH)]
    xcS = [alloc(f"xcS{k}", [128, 1152], BF16) for k in range(NCH)]
    gS = [alloc(f"gS{k}", [128, 1024], BF16) for k in range(NCH)]
    VN = ["ln0_g", "ln0_b", "ln1_g", "ln1_b", "tm_k", "tm_v", "tm_r",
          "cm_k", "cm_r", "time_decay", "time_first", "ew", "eu"]
    vecs = {nm: alloc("v_" + nm, [128, 8], F32) for nm in VN}
    sgnT = alloc("sgnT", [128, 64], BF16)
    idn = alloc("idn", [128, 128], F32)
    m01b = alloc("m01b", [128, HALO], BF16)
    m01u = alloc("m01u", [128, HALO], U8)
    lmB = alloc("lmB", [128, HALO], F32)
    bffn_sb = alloc("bffn_sb", [8, C], BF16)
    selb = alloc("selb", [8, 1024], BF16)
    ones_f = alloc("ones_f", [1, 128], F32)
    onescol = alloc("onescol", [128, 1], F32)
    onescol_b = alloc("onescol_b", [128, 1], BF16)
    epscol = alloc("epscol", [128, 1], F32)
    tinycol = alloc("tinycol", [128, 1], F32)
    zerot = alloc("zerot", [128, 128], BF16)
    Ast = alloc("Ast", [128, 8], F32)
    Bst = alloc("Bst", [128, 8], F32)
    xacar = alloc("xacar", [128, 8], BF16)
    xccar = alloc("xccar", [128, 8], BF16)

    AR_F32 = 15744
    ar_h = nc.alloc_sbuf_tensor("arena", [128, AR_F32], F32)
    ar16 = ar_h.bitcast(BF16)
    ar8 = ar_h.bitcast(U8)
    NSLOT = 3
    WPOOL = [alloc(f"wp{i}", [128, 8192], BF16) for i in range(NSLOT)]

    class Arena:
        def __init__(self):
            self.off = 0

        def take(self, cols, dt):
            eb = {BF16: 2, U8: 1}.get(dt, 4)
            self.off = (self.off + 63) & ~63
            o = self.off
            self.off += cols * eb
            assert self.off <= AR_F32 * 4, f"arena overflow {self.off}"
            if dt == F32:
                h = ar_h[:, o // 4:o // 4 + cols]
            elif dt == U8:
                h = ar8[:, o:o + cols]
            else:
                h = ar16[:, o // 2:o // 2 + cols]
            return Buf(h, "ARENA", o, eb, cols)

    arena = Arena()
    pb = [Buf(nc.alloc_psum_tensor(f"pbank{i}", [128, 512], F32),
              f"pbank{i}", 0, 4, 512) for i in range(8)]

    P = Prog()
    E = P.emit

    def piv(b):
        return [b.iv(0, 512)]

    tw = list(TAPS_WANT)
    dbg = None
    if tw:
        dbg = nc.dram_tensor("dbg", [128, 516 * len(tw)], F32,
                             kind="ExternalOutput").ap()

    def tap(tag, ap, ivs):
        if tag not in tw:
            return
        i = tw.index(tag)
        p_, w_ = ap.shape[0], ap.shape[1]
        dst = dbg[0:p_, i * 516:i * 516 + w_]
        E("gpsimd", (lambda a_, d_: lambda e: e.dma_start(out=d_, in_=a_))(ap, dst),
          reads=list(ivs), writes=[], dma="tapq")

    it_a = arena.take(128, F32)
    it_b = arena.take(128, F32)
    vstage = arena.take(128, F32)
    sstage = arena.take(1024, F32)
    mstage = arena.take(HALO, F32)
    spi = arena.take(1024, F32)
    m01f = arena.take(HALO, F32)

    E("gpsimd", lambda e: e.iota(it_a.h, pattern=[[0, 128]], channel_multiplier=1,
                                 allow_small_or_imprecise_dtypes=True),
      writes=[it_a.iv()])
    E("gpsimd", lambda e: e.iota(it_b.h, pattern=[[1, 128]], channel_multiplier=0,
                                 allow_small_or_imprecise_dtypes=True),
      writes=[it_b.iv()])
    E("vector", lambda e: e.tensor_tensor(idn.h[:], it_a.h, it_b.h, OP.is_equal),
      reads=[it_a.iv(), it_b.iv()], writes=[idn.iv()])
    for pair in [(ones_f, 1.0), (onescol, 1.0), (onescol_b, 1.0), (zerot, 0.0),
                 (Ast, 0.0), (Bst, 0.0), (epscol, EPS), (tinycol, 1e-30)]:
        E("gpsimd", (lambda bb, vv: lambda e: e.memset(bb.h[:], vv))(*pair),
          writes=[pair[0].iv()])

    def init_vec(nm):
        dst = vecs[nm]
        src = vec_in[nm].rearrange("(c p) -> c p", p=128)
        E("sync", lambda e: e.dma_start(out=vstage.h[0:8, 0:128], in_=src),
          writes=[vstage.iv()], dma="vst")
        E("tensor", lambda e: e.transpose(pb[0].h[0:128, 0:8],
                                          vstage.h[0:8, 0:128],
                                          idn.h[0:8, 0:8]),
          reads=[vstage.iv(), idn.iv()], writes=piv(pb[0]))
        E("vector", lambda e: e.tensor_copy(dst.h[:], pb[0].h[0:128, 0:8]),
          reads=piv(pb[0]), writes=[dst.iv()])

    for nm in ["ln0_g", "ln0_b", "ln1_g", "ln1_b", "tm_k", "tm_v", "tm_r",
               "cm_k", "cm_r", "time_decay", "time_first"]:
        init_vec(nm)
    td, tf = vecs["time_decay"], vecs["time_first"]
    E("scalar", lambda e: e.activation(td.h[:], td.h[:], AF.Exp),
      reads=[td.iv()], writes=[td.iv()])
    E("scalar", lambda e: e.activation(vecs["ew"].h[:], td.h[:], AF.Exp, scale=-1.0),
      reads=[td.iv()], writes=[vecs["ew"].iv()])
    E("scalar", lambda e: e.activation(vecs["eu"].h[:], tf.h[:], AF.Exp),
      reads=[tf.iv()], writes=[vecs["eu"].iv()])

    E("gpsimd", lambda e: e.iota(spi.h[0:8, 0:1024], pattern=[[0, 1024]],
                                 channel_multiplier=1,
                                 allow_small_or_imprecise_dtypes=True),
      writes=[spi.iv()])
    E("gpsimd", lambda e: e.iota(sstage.h[0:8, 0:1024], pattern=[[1, 8], [0, 128]],
                                 channel_multiplier=0,
                                 allow_small_or_imprecise_dtypes=True),
      writes=[sstage.iv()])
    E("vector", lambda e: e.tensor_tensor(selb.h[:], spi.h[0:8, 0:1024],
                                          sstage.h[0:8, 0:1024], OP.is_equal),
      reads=[spi.iv(), sstage.iv()], writes=[selb.iv()])

    E("sync", lambda e: e.dma_start(out=sstage.h[0:8, 0:1024], in_=sigs),
      writes=[sstage.iv()], dma="sgq")
    E("scalar", lambda e: e.activation(sstage.h[0:8, 0:1024],
                                       sstage.h[0:8, 0:1024], AF.Sign),
      reads=[sstage.iv()], writes=[sstage.iv()])

    def init_sgn(kc):
        E("tensor", lambda e: e.transpose(pb[1].h[0:128, 0:8],
                                          sstage.h[0:8, kc * 128:(kc + 1) * 128],
                                          idn.h[0:8, 0:8]),
          reads=[sstage.iv(), idn.iv()], writes=piv(pb[1]))
        E("vector", lambda e: e.tensor_copy(sgnT.h[:, kc * 8:(kc + 1) * 8],
                                            pb[1].h[0:128, 0:8]),
          reads=piv(pb[1]), writes=[sgnT.iv(kc * 8, kc * 8 + 8)])

    for kc in range(NCH):
        init_sgn(kc)

    E("gpsimd", lambda e: e.dma_start(out=bffn_sb.h[:], in_=bffn),
      writes=[bffn_sb.iv()], dma="bffn")

    E("sync", lambda e: e.dma_start(out=mstage.h[0:1, 0:HALO], in_=mrow),
      writes=[mstage.iv()], dma="msq")
    E("tensor", lambda e: e.matmul(pb[2].h[:, 0:HALO], ones_f.h[:],
                                   mstage.h[0:1, 0:HALO], start=True, stop=True),
      reads=[mstage.iv(), ones_f.iv()], writes=piv(pb[2]))
    E("vector", lambda e: e.tensor_copy(m01f.h[:], pb[2].h[:, 0:HALO]),
      reads=piv(pb[2]), writes=[m01f.iv()])
    E("sync", lambda e: e.dma_start(out=mstage.h[0:1, 0:HALO], in_=lrow),
      writes=[mstage.iv()], dma="msq")
    E("tensor", lambda e: e.matmul(pb[3].h[:, 0:HALO], ones_f.h[:],
                                   mstage.h[0:1, 0:HALO], start=True, stop=True),
      reads=[mstage.iv(), ones_f.iv()], writes=piv(pb[3]))
    E("vector", lambda e: e.tensor_copy(lmB.h[:], pb[3].h[:, 0:HALO]),
      reads=piv(pb[3]), writes=[lmB.iv()])
    E("vector", lambda e: e.tensor_copy(m01b.h[:], m01f.h[:]),
      reads=[m01f.iv()], writes=[m01b.iv()])
    E("vector", lambda e: e.tensor_copy(m01u.h[:], m01f.h[:]),
      reads=[m01f.iv()], writes=[m01u.iv()])

    wrot = [0]

    def wload(src):
        i = wrot[0] % NSLOT
        wrot[0] += 1
        s = WPOOL[i]
        E("gpsimd", (lambda ss, sap: lambda e: e.dma_start(out=ss.h[:, 0:8192],
                                                           in_=sap))(s, src),
          writes=[s.iv(0, 8192)], dma=f"wp{i}")
        return s

    global PHASE_MARKS
    PHASE_MARKS = {}

    def mark(name):
        PHASE_MARKS[name] = len(P.ops)

    def do_st(sti, st):
        x0, dom, gdom = st["x0"], st["dom"], st["gdom"]
        nx = st["x1"] - x0

        arena.off = 0
        xt = [arena.take(1024, F32) for _ in range(6)]
        sql = [arena.take(1024, F32) for _ in range(3)]
        xnt = [arena.take(1024, F32) for _ in range(3)]
        stc = [[arena.take(1, F32) for _ in range(5)] for _ in range(3)]

        def load_dma(ti):
            r0 = x0 + ti * 128
            xb = xt[ti % 6]
            E("sync", lambda e: e.dma_start(out=xb.h[:], in_=xsl[r0:r0 + 128, :]),
              writes=[xb.iv()], dma=f"xt{ti % 6}")

        def load_s1(ti):
            par = ti % 3
            xb, sq = xt[ti % 6], sql[par]
            ssq, m, v, rst, nmr = stc[par]
            E("scalar", lambda e: e.activation(out=sq.h, in_=xb.h, func=AF.Square,
                                               accum_out=ssq.h[:]),
              reads=[xb.iv()], writes=[sq.iv(), ssq.iv()])
            E("vector", lambda e: e.reduce_sum(m.h, xb.h, axis=AX),
              reads=[xb.iv()], writes=[m.iv()])
            E("vector", lambda e: e.tensor_scalar(m.h, m.h, 1.0 / C, None,
                                                  op0=OP.mult),
              reads=[m.iv()], writes=[m.iv()])
            E("vector", lambda e: e.tensor_tensor(nmr.h, m.h, m.h, OP.mult),
              reads=[m.iv()], writes=[nmr.iv()])
            E("vector", lambda e: e.scalar_tensor_tensor(
                v.h, ssq.h, 1.0 / C, nmr.h, op0=OP.mult, op1=OP.subtract),
              reads=[ssq.iv(), nmr.iv()], writes=[v.iv()])
            E("scalar", lambda e: e.activation(v.h, v.h, AF.Ln, bias=epscol.h[:]),
              reads=[v.iv(), epscol.iv()], writes=[v.iv()])
            E("scalar", lambda e: e.activation(rst.h, v.h, AF.Exp, scale=-0.5),
              reads=[v.iv()], writes=[rst.iv()])

        def load_s2(ti):
            par = ti % 3
            xb, sq, xh = xt[ti % 6], sql[par], xnt[par]
            ssq, m, v, rst, nmr = stc[par]
            E("vector", lambda e: e.scalar_tensor_tensor(
                nmr.h, m.h, -1.0, rst.h, op0=OP.mult, op1=OP.mult),
              reads=[m.iv(), rst.iv()], writes=[nmr.iv()])
            E("vector", lambda e: e.tensor_scalar(
                xh.h, xb.h, rst.h[:], nmr.h[:], op0=OP.mult, op1=OP.add),
              reads=[xb.iv(), rst.iv(), nmr.iv()], writes=[xh.iv()])
            E("scalar", lambda e: e.activation(out=sq.h, in_=xh.h, func=AF.Square,
                                               accum_out=ssq.h[:]),
              reads=[xh.iv()], writes=[sq.iv(), ssq.iv()])
            E("vector", lambda e: e.tensor_scalar(v.h, ssq.h, 1.0 / C, None,
                                                  op0=OP.mult),
              reads=[ssq.iv()], writes=[v.iv()])
            E("scalar", lambda e: e.activation(v.h, v.h, AF.Ln, bias=epscol.h[:]),
              reads=[v.iv(), epscol.iv()], writes=[v.iv()])
            E("scalar", lambda e: e.activation(rst.h, v.h, AF.Exp, scale=-0.5),
              reads=[v.iv()], writes=[rst.iv()])

        def load_tile(ti):
            r0 = x0 + ti * 128
            par = ti % 3
            sq, xh = sql[par], xnt[par]
            ssq, m, v, rst, nmr = stc[par]

            def tile_kc(kc):
                if r0 >= dom:
                    pbx = pb[kc % 2]
                    E("tensor", lambda e: e.transpose(
                        pbx.h[:, 0:128], xh.h[:, kc * 128:(kc + 1) * 128],
                        idn.h[:]),
                      reads=[xh.iv(), idn.iv()], writes=piv(pbx))
                    E("vector", lambda e: e.tensor_copy(
                        XN[kc].h[:, r0 - dom:r0 - dom + 128], pbx.h[:, 0:128]),
                      reads=piv(pbx),
                      writes=[XN[kc].iv(r0 - dom, r0 - dom + 128)])
                pby = pb[4 + kc % 2]
                E("scalar", lambda e: e.activation(
                    sq.h[:, kc * 128:(kc + 1) * 128],
                    xh.h[:, kc * 128:(kc + 1) * 128], AF.Copy, scale=rst.h[:]),
                  reads=[xh.iv(), rst.iv()],
                  writes=[sq.iv(kc * 128, kc * 128 + 128)])
                E("tensor", lambda e: e.transpose(
                    pby.h[:, 0:128], sq.h[:, kc * 128:(kc + 1) * 128], idn.h[:]),
                  reads=[sq.iv(kc * 128, kc * 128 + 128), idn.iv()],
                  writes=piv(pby))
                E("vector", lambda e: e.tensor_scalar(
                    xaT[kc].h[:, r0 - x0:r0 - x0 + 128], pby.h[:, 0:128],
                    vecs["ln0_g"].h[:, kc:kc + 1],
                    vecs["ln0_b"].h[:, kc:kc + 1], op0=OP.mult, op1=OP.add),
                  reads=piv(pby) + [vecs["ln0_g"].iv(), vecs["ln0_b"].iv()],
                  writes=[xaT[kc].iv(r0 - x0, r0 - x0 + 128)])
                if sti == 0 and r0 < HALO:
                    E("vector", lambda e: e.tensor_tensor(
                        xaT[kc].h[:, r0 - x0:r0 - x0 + 128],
                        xaT[kc].h[:, r0 - x0:r0 - x0 + 128],
                        m01b.h[:, r0:r0 + 128], OP.mult),
                      reads=[xaT[kc].iv(r0 - x0, r0 - x0 + 128), m01b.iv()],
                      writes=[xaT[kc].iv(r0 - x0, r0 - x0 + 128)])

            for kc in range(NCH):
                tile_kc(kc)

        NT = nx // 128
        for ti in range(min(5, NT)):
            load_dma(ti)
        for w in range(NT + 2):
            if w < NT:
                load_s1(w)
            if 0 <= w - 1 < NT:
                load_s2(w - 1)
            if 0 <= w - 2 < NT:
                load_tile(w - 2)
            if w + 5 < NT:
                load_dma(w + 5)
        if sti == 0:
            tap("xn0", XN[0].h[:, 0:512], [XN[0].iv(0, 512)])
            tap("xa0", xaT[0].h[:, 0:512], [xaT[0].iv(0, 512)])
        if sti == 0:
            def xa_car(kc):
                E("vector", lambda e: e.tensor_copy(
                    xacar.h[:, kc:kc + 1], xaT[kc].h[:, 1279:1280]),
                  reads=[xaT[kc].iv(1279, 1280)], writes=[xacar.iv(kc, kc + 1)])
            for kc in range(NCH):
                xa_car(kc)

        if ONLY_LOAD:
            return
        mark(f"ffn{sti}")
        arena.off = 0
        outac = [Buf(rwk[mc].h[:, 0:1024], f"rwk{mc}", 0, 2, 1024)
                 for mc in range(NCH)]
        M8t = [arena.take(512, BF16) for _ in range(2)]
        mbu = arena.take(16 * 512, U8)
        scsb = arena.take(512, F32)
        scT = arena.take(8, F32)
        mxc = arena.take(1, F32)
        mTf = arena.take(8, F32)
        sil = [arena.take(512, F32) for _ in range(2)]
        fb = st["rt"][0][0] - dom
        ntile = len(st["rt"])

        def ffn_scores(tix, t0, t1):
            L = t1 - t0
            xb = fb + tix * 512
            M8 = M8t[tix]

            def scores_kc(kc):
                E("tensor", lambda e: e.matmul(
                    pb[7].h[0:8, 0:L], sgnT.h[:, kc * 8:(kc + 1) * 8],
                    XN[kc].h[:, xb:xb + L],
                    start=(kc == 0), stop=(kc == 7)),
                  reads=[sgnT.iv(kc * 8, kc * 8 + 8), XN[kc].iv(xb, xb + L)],
                  writes=piv(pb[7]))

            for kc in range(NCH):
                scores_kc(kc)
            E("vector", lambda e: e.tensor_copy(scsb.h[0:8, 0:L],
                                                pb[7].h[0:8, 0:L]),
              reads=piv(pb[7]), writes=[scsb.iv()])

            def mask_j(j):
                E("tensor", lambda e: e.transpose(
                    pb[6].h[0:128, 0:8], scsb.h[0:8, j * 128:(j + 1) * 128],
                    idn.h[0:8, 0:8]),
                  reads=[scsb.iv(), idn.iv()], writes=piv(pb[6]))
                E("vector", lambda e: e.tensor_copy(scT.h[0:128, 0:8],
                                                    pb[6].h[0:128, 0:8]),
                  reads=piv(pb[6]), writes=[scT.iv()])
                E("vector", lambda e: e.reduce_max(mxc.h, scT.h[0:128, 0:8],
                                                   axis=AX),
                  reads=[scT.iv()], writes=[mxc.iv()])
                E("vector", lambda e: e.tensor_scalar(
                    mTf.h[0:128, 0:8], scT.h[0:128, 0:8], mxc.h[:], None,
                    op0=OP.is_ge),
                  reads=[scT.iv(), mxc.iv()], writes=[mTf.iv()])
                E("tensor", lambda e: e.transpose(
                    pb[5].h[0:8, 0:128], mTf.h[0:128, 0:8], idn.h[:]),
                  reads=[mTf.iv(), idn.iv()], writes=piv(pb[5]))
                E("vector", lambda e: e.tensor_copy(
                    M8.h[0:8, j * 128:(j + 1) * 128], pb[5].h[0:8, 0:128]),
                  reads=piv(pb[5]), writes=[M8.iv(j * 128, (j + 1) * 128)])

            for j in range(L // 128):
                mask_j(j)

            def pred_e(e2):
                bnk = pb[(e2 * 2 + tix) % 4]
                off = (e2 * 2 + tix) * 512
                E("tensor", lambda e: e.matmul(
                    bnk.h[:, 0:L], selb.h[0:8, e2 * 128:(e2 + 1) * 128],
                    M8.h[0:8, 0:L], start=True, stop=True),
                  reads=[M8.iv(), selb.iv()], writes=piv(bnk))
                E("vector", lambda e: e.tensor_copy(
                    mbu.h[:, off:off + L], bnk.h[:, 0:L]),
                  reads=piv(bnk), writes=[mbu.iv(off, off + L)])

            for e2 in range(8):
                pred_e(e2)

        for tix0, (t0, t1) in enumerate(st["rt"]):
            ffn_scores(tix0, t0, t1)

        def expert(e2):
            ws = wload(wffn[e2])

            def etile(tix):
                xb = fb + tix * 512

                def mm(mc, kc):
                    E("tensor", lambda e: e.matmul(
                        pb[mc].h[:, 0:512],
                        ws.h[:, kc * 1024 + mc * 128:kc * 1024 + (mc + 1) * 128],
                        XN[kc].h[:, xb:xb + 512],
                        start=(kc == 0), stop=(kc == 7)),
                      reads=[ws.iv(kc * 1024, (kc + 1) * 1024),
                             XN[kc].iv(xb, xb + 512)],
                      writes=piv(pb[mc]))

                for mc in range(NCH):
                    for kc in range(NCH):
                        mm(mc, kc)

                def evac(mc):
                    off = (e2 * 2 + tix) * 512
                    E("vector", lambda e: e.copy_predicated(
                        outac[mc].h[:, tix * 512:(tix + 1) * 512],
                        mbu.h[:, off:off + 512], pb[mc].h[:, 0:512]),
                      reads=piv(pb[mc]) + [mbu.iv(off, off + 512),
                                           outac[mc].iv(tix * 512,
                                                        (tix + 1) * 512)],
                      writes=[outac[mc].iv(tix * 512, (tix + 1) * 512)])

                for mc in range(NCH):
                    evac(mc)

            for tix in range(ntile):
                etile(tix)

        for e2 in reversed(range(8)):
            expert(e2)

        def bias_silu(tix, mc):
            t0 = st["rt"][tix][0]
            gcol = t0 - gdom
            s = sil[mc % 2]
            E("tensor", lambda e: e.matmul(
                pb[mc].h[:, 0:512], bffn_sb.h[0:8, mc * 128:(mc + 1) * 128],
                M8t[tix].h[0:8, 0:512], start=True, stop=True),
              reads=[bffn_sb.iv(), M8t[tix].iv()], writes=piv(pb[mc]))
            E("vector", lambda e: e.tensor_tensor(
                s.h[:, 0:512], outac[mc].h[:, tix * 512:(tix + 1) * 512],
                pb[mc].h[:, 0:512], OP.add),
              reads=[outac[mc].iv(tix * 512, (tix + 1) * 512)] + piv(pb[mc]),
              writes=[s.iv()])
            E("scalar", lambda e: e.activation(
                gS[mc].h[:, gcol:gcol + 512], s.h[:, 0:512], AF.Silu),
              reads=[s.iv()], writes=[gS[mc].iv(gcol, gcol + 512)])

        for tix0 in range(ntile):
            for mc in range(NCH):
                bias_silu(tix0, mc)
        if sti == 0:
            tap("g0", gS[0].h[:, 0:512], [gS[0].iv(0, 512)])
            tap("m8", M8t[0].h[0:8, 0:512], [M8t[0].iv(0, 512)])

        mark(f"kvr{sti}")
        arena.off = 0
        mx_ = {q: [arena.take(8 * 512, BF16) for _ in range(2)] for q in "kvr"}
        dsc = arena.take(512, BF16)
        rwt = Buf(dsc.h[:, 0:256], "ARENA", dsc.off, 2, 256)
        ekb = arena.take(512, F32)
        ekvb = arena.take(512, F32)
        numb = arena.take(512, F32)
        exrb = arena.take(512, F32)
        Abuf = arena.take(513, F32)
        Bbuf = arena.take(513, F32)
        wkS = {"k": wload(wdram["Wk"]), "v": wload(wdram["Wv"]),
               "r": wload(wdram["Wr"])}
        tmv = {"k": "tm_k", "v": "tm_v", "r": "tm_r"}
        if sti == 0:
            def mset(q, kc):
                E("gpsimd", lambda e: e.memset(
                    mx_[q][0].h[:, kc * 512:kc * 512 + 1], 0.0),
                  writes=[mx_[q][0].iv(kc * 512, kc * 512 + 1)])
            for q in "kvr":
                for kc in range(NCH):
                    mset(q, kc)

        def kvr_mixes(tix, t0, t1):
            L = t1 - t0
            lc0 = t0 - x0
            first0 = (sti == 0 and tix == 0)
            firstC = (sti == 1 and tix == 0)
            s0 = 1 if (first0 or firstC) else 0
            pp = tix % 2

            def mix_kc(kc):
                cur = xaT[kc].h[:, lc0 + s0:lc0 + L]
                shf = xaT[kc].h[:, lc0 + s0 - 1:lc0 + L - 1]
                riv = [xaT[kc].iv(max(lc0 - 1, 0), lc0 + L)]
                E("gpsimd", lambda e: e.tensor_tensor(
                    dsc.h[:, 0:L - s0], cur, shf, OP.subtract),
                  reads=riv, writes=[dsc.iv()])

                def mix_q(q):
                    E("vector", lambda e: e.scalar_tensor_tensor(
                        mx_[q][pp].h[:, kc * 512 + s0:kc * 512 + L],
                        dsc.h[:, 0:L - s0], vecs[tmv[q]].h[:, kc:kc + 1], shf,
                        op0=OP.mult, op1=OP.add),
                      reads=[dsc.iv(), vecs[tmv[q]].iv()] + riv,
                      writes=[mx_[q][pp].iv(kc * 512 + s0, kc * 512 + L)])

                for q in "kvr":
                    mix_q(q)
                if firstC:
                    E("vector", lambda e: e.tensor_tensor(
                        dsc.h[:, 0:1], xaT[kc].h[:, 0:1],
                        xacar.h[:, kc:kc + 1], OP.subtract),
                      reads=[xaT[kc].iv(0, 1), xacar.iv(kc, kc + 1)],
                      writes=[dsc.iv(0, 2)])

                    def mix0_q(q):
                        E("vector", lambda e: e.scalar_tensor_tensor(
                            mx_[q][pp].h[:, kc * 512:kc * 512 + 1],
                            dsc.h[:, 0:1], vecs[tmv[q]].h[:, kc:kc + 1],
                            xacar.h[:, kc:kc + 1], op0=OP.mult, op1=OP.add),
                          reads=[dsc.iv(0, 2), vecs[tmv[q]].iv(),
                                 xacar.iv(kc, kc + 1)],
                          writes=[mx_[q][pp].iv(kc * 512, kc * 512 + 1)])

                    for q in "kvr":
                        mix0_q(q)

            for kc in range(NCH):
                mix_kc(kc)

        def kvr_tile(tix, t0, t1):
            L = t1 - t0
            first0 = (sti == 0 and tix == 0)
            pp = tix % 2

            def kvr_mc(mc):
                par = mc % 2
                bk, bv, br = pb[par * 3], pb[par * 3 + 1], pb[par * 3 + 2]

                def proj(q, bnk, kc):
                    E("tensor", lambda e: e.matmul(
                        bnk.h[:, 0:L],
                        wkS[q].h[:, kc * 1024 + mc * 128:kc * 1024 + (mc + 1) * 128],
                        mx_[q][pp].h[:, kc * 512:kc * 512 + L],
                        start=(kc == 0), stop=(kc == 7)),
                      reads=[wkS[q].iv(kc * 1024, (kc + 1) * 1024),
                             mx_[q][pp].iv(kc * 512, kc * 512 + L)],
                      writes=piv(bnk))

                for q, bnk in [("k", bk), ("v", bv), ("r", br)]:
                    for kc in range(NCH):
                        proj(q, bnk, kc)
                if first0:
                    E("vector", lambda e: e.scalar_tensor_tensor(
                        ekb.h[:, 0:L], bk.h[:, 0:L], 1.0, lmB.h[:, 0:L],
                        op0=OP.mult, op1=OP.add),
                      reads=piv(bk) + [lmB.iv()], writes=[ekb.iv()])
                    E("scalar", lambda e: e.activation(ekb.h[:, 0:L],
                                                       ekb.h[:, 0:L], AF.Exp),
                      reads=[ekb.iv()], writes=[ekb.iv()])
                else:
                    E("scalar", lambda e: e.activation(ekb.h[:, 0:L],
                                                       bk.h[:, 0:L], AF.Exp),
                      reads=piv(bk), writes=[ekb.iv()])
                E("vector", lambda e: e.tensor_tensor(
                    ekvb.h[:, 0:L], bv.h[:, 0:L], ekb.h[:, 0:L], OP.mult),
                  reads=piv(bv) + [ekb.iv()], writes=[ekvb.iv()])
                E("scalar", lambda e: e.activation(exrb.h[:, 0:L], br.h[:, 0:L],
                                                   AF.Sigmoid),
                  reads=piv(br), writes=[exrb.iv()])

                def scan(buf, data, stv):
                    E("vector", lambda e: e.tensor_copy(buf.h[:, 0:1],
                                                        stv.h[:, mc:mc + 1]),
                      reads=[stv.iv(mc, mc + 1)], writes=[buf.iv(0, 1)])
                    E("vector", lambda e: e.tensor_tensor_scan(
                        buf.h[:, 1:L + 1],
                        vecs["ew"].h[:, mc:mc + 1].broadcast_to([128, L]),
                        data.h[:, 0:L], initial=stv.h[:, mc:mc + 1],
                        op0=OP.mult, op1=OP.add),
                      reads=[data.iv(), stv.iv(mc, mc + 1), vecs["ew"].iv()],
                      writes=[buf.iv(1, L + 1)])
                    E("vector", lambda e: e.tensor_copy(stv.h[:, mc:mc + 1],
                                                        buf.h[:, L:L + 1]),
                      reads=[buf.iv(L, L + 1)], writes=[stv.iv(mc, mc + 1)])

                if sti == 0 and tix == 1 and mc == 0:
                    tap("ek", ekb.h[:, 0:512], [ekb.iv()])
                    tap("ekv", ekvb.h[:, 0:512], [ekvb.iv()])
                scan(Abuf, ekvb, Ast)
                scan(Bbuf, ekb, Bst)
                if sti == 0 and tix == 1 and mc == 0:
                    tap("Abuf", Abuf.h[:, 0:513], [Abuf.iv()])
                E("vector", lambda e: e.scalar_tensor_tensor(
                    numb.h[:, 0:L], ekvb.h[:, 0:L],
                    vecs["eu"].h[:, mc:mc + 1], Abuf.h[:, 0:L],
                    op0=OP.mult, op1=OP.add),
                  reads=[ekvb.iv(), vecs["eu"].iv(), Abuf.iv(0, L)],
                  writes=[numb.iv()])
                E("vector", lambda e: e.scalar_tensor_tensor(
                    ekb.h[:, 0:L], ekb.h[:, 0:L],
                    vecs["eu"].h[:, mc:mc + 1], Bbuf.h[:, 0:L],
                    op0=OP.mult, op1=OP.add),
                  reads=[ekb.iv(), vecs["eu"].iv(), Bbuf.iv(0, L)],
                  writes=[ekb.iv()])
                E("scalar", lambda e: e.activation(ekvb.h[:, 0:L],
                                                   ekb.h[:, 0:L], AF.Ln,
                                                   bias=tinycol.h[:]),
                  reads=[ekb.iv(), tinycol.iv()], writes=[ekvb.iv()])
                E("scalar", lambda e: e.activation(ekvb.h[:, 0:L],
                                                   ekvb.h[:, 0:L], AF.Exp,
                                                   scale=-1.0),
                  reads=[ekvb.iv()], writes=[ekvb.iv()])
                E("vector", lambda e: e.tensor_tensor(
                    numb.h[:, 0:L], numb.h[:, 0:L], ekvb.h[:, 0:L], OP.mult),
                  reads=[numb.iv(), ekvb.iv()], writes=[numb.iv()])
                ds = max(t0, dom) - dom
                so = max(t0, dom) - t0
                n = t1 - max(t0, dom)
                if first0:
                    E("vector", lambda e: e.tensor_tensor(
                        rwt.h, numb.h[:, 0:L], exrb.h[:, 0:L], OP.mult),
                      reads=[numb.iv(), exrb.iv()], writes=[rwt.iv()])
                    E("vector", lambda e: e.tensor_copy(
                        rwk[mc].h[:, ds:ds + n], zerot.h[:, 0:n]),
                      reads=[zerot.iv()], writes=[rwk[mc].iv(ds, ds + n)])
                    E("vector", lambda e: e.copy_predicated(
                        rwk[mc].h[:, ds:ds + n], m01u.h[:, so:so + n],
                        rwt.h[:, so:so + n]),
                      reads=[rwt.iv(), m01u.iv(), rwk[mc].iv(ds, ds + n)],
                      writes=[rwk[mc].iv(ds, ds + n)])
                else:
                    E("gpsimd", lambda e: e.tensor_tensor(
                        rwk[mc].h[:, ds:ds + n], numb.h[:, 0:n],
                        exrb.h[:, 0:n], OP.mult),
                      reads=[numb.iv(), exrb.iv()],
                      writes=[rwk[mc].iv(ds, ds + n)])

            for mc in range(NCH):
                kvr_mc(mc)

        ktiles = st["kvr"]
        kvr_mixes(0, *ktiles[0])
        for tix, (t0, t1) in enumerate(ktiles):
            if tix + 1 < len(ktiles):
                kvr_mixes(tix + 1, *ktiles[tix + 1])
            kvr_tile(tix, t0, t1)
        if sti == 0:
            tap("rwk0", rwk[0].h[:, 0:512], [rwk[0].iv(0, 512)])

        mark(f"wo{sti}")
        arena.off = 0
        sqx = [arena.take(512, F32) for _ in range(2)]
        t1s = arena.take(512, F32)
        t1g = arena.take(512, F32)
        Rb = arena.take(512, F32)
        M2b = arena.take(512, F32)
        rows = {nm: arena.take(512, F32) for nm in ["ssq", "sx", "rstd", "m2"]}
        kkq = arena.take(16 * 512, BF16)
        kvacc = [Buf(rwk[mc].h[:, 0:1024], f"rwk{mc}", 0, 2, 1024)
                 for mc in range(NCH)]
        mxck = arena.take(8 * 1024, BF16)
        rls = [arena.take(512, BF16) for _ in range(2)]
        kvf = arena.take(512, F32)
        dsc3 = arena.take(512, BF16)
        mxcr = Buf(ar16[:, (kkq.off + 8192) // 2:(kkq.off + 8192) // 2 + 4096],
                   "ARENA", kkq.off + 8192, 2, 4096)
        dsc2 = arena.take(512, BF16)
        ntile = len(st["rt"])
        woS = wload(wdram["Wo"])
        wcrS = wload(wdram["Wcr"])
        ckw = {0: [wload(wdram["Wck"][0]), None]}

        def wo_mm_tile(g0, g1):
            L = g1 - g0
            bcol = g0 - dom

            def wo_mc(mc):
                bnk = pb[6 + mc % 2]

                def mm(kc):
                    E("tensor", lambda e: e.matmul(
                        bnk.h[:, 0:L],
                        woS.h[:, kc * 1024 + mc * 128:kc * 1024 + (mc + 1) * 128],
                        rwk[kc].h[:, bcol:bcol + L],
                        start=(kc == 0), stop=(kc == 7)),
                      reads=[woS.iv(kc * 1024, (kc + 1) * 1024),
                             rwk[kc].iv(bcol, bcol + L)],
                      writes=piv(bnk))

                for kc in range(NCH):
                    mm(kc)
                E("vector", lambda e: e.tensor_tensor(
                    XN[mc].h[:, bcol:bcol + L], bnk.h[:, 0:L],
                    XN[mc].h[:, bcol:bcol + L], OP.add),
                  reads=piv(bnk) + [XN[mc].iv(bcol, bcol + L)],
                  writes=[XN[mc].iv(bcol, bcol + L)])

            for mc in range(NCH):
                wo_mc(mc)

        def wo_tile(gix, g0, g1):
            L = g1 - g0
            bcol = g0 - dom

            def stats_kc(kc):
                sq = sqx[kc % 2]
                E("scalar", lambda e: e.activation(
                    sq.h[:, 0:L], XN[kc].h[:, bcol:bcol + L], AF.Square),
                  reads=[XN[kc].iv(bcol, bcol + L)], writes=[sq.iv()])
                E("tensor", lambda e: e.matmul(
                    pb[2].h[0:1, 0:L], onescol.h[:], sq.h[:, 0:L],
                    start=(kc == 0), stop=(kc == 7)),
                  reads=[sq.iv(), onescol.iv()], writes=piv(pb[2]))
                E("tensor", lambda e: e.matmul(
                    pb[3].h[0:1, 0:L], onescol_b.h[:],
                    XN[kc].h[:, bcol:bcol + L],
                    start=(kc == 0), stop=(kc == 7)),
                  reads=[XN[kc].iv(bcol, bcol + L), onescol_b.iv()],
                  writes=piv(pb[3]))

            for kc in range(NCH):
                stats_kc(kc)
            E("vector", lambda e: e.tensor_copy(rows["ssq"].h[0:1, 0:L],
                                                pb[2].h[0:1, 0:L]),
              reads=piv(pb[2]), writes=[rows["ssq"].iv()])
            E("vector", lambda e: e.tensor_copy(rows["sx"].h[0:1, 0:L],
                                                pb[3].h[0:1, 0:L]),
              reads=piv(pb[3]), writes=[rows["sx"].iv()])
            E("vector", lambda e: e.tensor_scalar(
                rows["sx"].h[0:1, 0:L], rows["sx"].h[0:1, 0:L], 1.0 / C, None,
                op0=OP.mult),
              reads=[rows["sx"].iv()], writes=[rows["sx"].iv()])
            E("vector", lambda e: e.tensor_tensor(
                rows["m2"].h[0:1, 0:L], rows["sx"].h[0:1, 0:L],
                rows["sx"].h[0:1, 0:L], OP.mult),
              reads=[rows["sx"].iv()], writes=[rows["m2"].iv()])
            E("vector", lambda e: e.scalar_tensor_tensor(
                rows["ssq"].h[0:1, 0:L], rows["ssq"].h[0:1, 0:L], 1.0 / C,
                rows["m2"].h[0:1, 0:L], op0=OP.mult, op1=OP.subtract),
              reads=[rows["ssq"].iv(), rows["m2"].iv()],
              writes=[rows["ssq"].iv()])
            E("scalar", lambda e: e.activation(
                rows["ssq"].h[0:1, 0:L], rows["ssq"].h[0:1, 0:L], AF.Ln,
                bias=epscol.h[0:1, 0:1]),
              reads=[rows["ssq"].iv(), epscol.iv()],
              writes=[rows["ssq"].iv()])
            E("scalar", lambda e: e.activation(
                rows["rstd"].h[0:1, 0:L], rows["ssq"].h[0:1, 0:L], AF.Exp,
                scale=-0.5),
              reads=[rows["ssq"].iv()], writes=[rows["rstd"].iv()])
            E("vector", lambda e: e.tensor_tensor(
                rows["m2"].h[0:1, 0:L], rows["sx"].h[0:1, 0:L],
                rows["rstd"].h[0:1, 0:L], OP.mult),
              reads=[rows["sx"].iv(), rows["rstd"].iv()],
              writes=[rows["m2"].iv()])
            E("tensor", lambda e: e.matmul(
                pb[4].h[:, 0:L], ones_f.h[:], rows["rstd"].h[0:1, 0:L],
                start=True, stop=True),
              reads=[rows["rstd"].iv(), ones_f.iv()], writes=piv(pb[4]))
            E("vector", lambda e: e.tensor_copy(Rb.h[:, 0:L], pb[4].h[:, 0:L]),
              reads=piv(pb[4]), writes=[Rb.iv()])
            E("tensor", lambda e: e.matmul(
                pb[5].h[:, 0:L], ones_f.h[:], rows["m2"].h[0:1, 0:L],
                start=True, stop=True),
              reads=[rows["m2"].iv(), ones_f.iv()], writes=piv(pb[5]))
            E("vector", lambda e: e.tensor_copy(M2b.h[:, 0:L], pb[5].h[:, 0:L]),
              reads=piv(pb[5]), writes=[M2b.iv()])

            def xc_kc(kc):
                eng = "vector"
                ts = t1s if kc % 2 else t1g
                E(eng, lambda e: e.tensor_tensor(
                    ts.h[:, 0:L], XN[kc].h[:, bcol:bcol + L], Rb.h[:, 0:L],
                    OP.mult),
                  reads=[XN[kc].iv(bcol, bcol + L), Rb.iv()], writes=[ts.iv()])
                E(eng, lambda e: e.tensor_tensor(
                    ts.h[:, 0:L], ts.h[:, 0:L], M2b.h[:, 0:L], OP.subtract),
                  reads=[ts.iv(), M2b.iv()], writes=[ts.iv()])
                E(eng, lambda e: e.tensor_scalar(
                    xcS[kc].h[:, bcol:bcol + L], ts.h[:, 0:L],
                    vecs["ln1_g"].h[:, kc:kc + 1],
                    vecs["ln1_b"].h[:, kc:kc + 1], op0=OP.mult, op1=OP.add),
                  reads=[ts.iv(), vecs["ln1_g"].iv(), vecs["ln1_b"].iv()],
                  writes=[xcS[kc].iv(bcol, bcol + L)])
                if sti == 0 and gix == 0:
                    E("vector", lambda e: e.tensor_tensor(
                        xcS[kc].h[:, 0:128], xcS[kc].h[:, 0:128],
                        m01b.h[:, 128:256], OP.mult),
                      reads=[xcS[kc].iv(0, 128), m01b.iv()],
                      writes=[xcS[kc].iv(0, 128)])

            for kc in range(NCH):
                xc_kc(kc)

        def cmix(dst, base, kc, bcol, firstC, cmvec, dscb):
            s0 = 1 if firstC else 0
            cur = xcS[kc].h[:, bcol + s0:bcol + 512]
            shf = xcS[kc].h[:, bcol + s0 - 1:bcol + 511]
            riv = [xcS[kc].iv(max(bcol - 1, 0), bcol + 512)]
            E("vector", lambda e: e.tensor_tensor(
                dscb.h[:, 0:512 - s0], cur, shf, OP.subtract),
              reads=riv, writes=[dscb.iv()])
            E("vector", lambda e: e.scalar_tensor_tensor(
                dst.h[:, base + s0:base + 512], dscb.h[:, 0:512 - s0],
                vecs[cmvec].h[:, kc:kc + 1], shf, op0=OP.mult, op1=OP.add),
              reads=[dscb.iv(), vecs[cmvec].iv()] + riv,
              writes=[dst.iv(base + s0, base + 512)])
            if firstC:
                E("vector", lambda e: e.tensor_tensor(
                    dscb.h[:, 0:1], xcS[kc].h[:, 0:1], xccar.h[:, kc:kc + 1],
                    OP.subtract),
                  reads=[xcS[kc].iv(0, 1), xccar.iv(kc, kc + 1)],
                  writes=[dscb.iv(0, 2)])
                E("vector", lambda e: e.scalar_tensor_tensor(
                    dst.h[:, base:base + 1], dscb.h[:, 0:1],
                    vecs[cmvec].h[:, kc:kc + 1], xccar.h[:, kc:kc + 1],
                    op0=OP.mult, op1=OP.add),
                  reads=[dscb.iv(0, 2), vecs[cmvec].iv(), xccar.iv(kc, kc + 1)],
                  writes=[dst.iv(base, base + 1)])

        def wcr_tile(tix, t0):
            bcol = t0 - dom
            firstC = (sti == 1 and tix == 0)
            for kc in range(NCH):
                cmix(mxcr, kc * 512, kc, bcol, firstC, "cm_r", dsc2)

            def wcr_mc(mc):
                bnk = pb[mc % 2]

                def mm(kc):
                    E("tensor", lambda e: e.matmul(
                        bnk.h[:, 0:512],
                        wcrS.h[:, kc * 1024 + mc * 128:kc * 1024 + (mc + 1) * 128],
                        mxcr.h[:, kc * 512:(kc + 1) * 512],
                        start=(kc == 0), stop=(kc == 7)),
                      reads=[wcrS.iv(kc * 1024, (kc + 1) * 1024),
                             mxcr.iv(kc * 512, (kc + 1) * 512)],
                      writes=piv(bnk))

                for kc in range(NCH):
                    mm(kc)
                E("scalar", lambda e: e.activation(
                    xaT[mc].h[:, tix * 512:(tix + 1) * 512], bnk.h[:, 0:512],
                    AF.Sigmoid),
                  reads=piv(bnk),
                  writes=[xaT[mc].iv(tix * 512, (tix + 1) * 512)])

            for mc in range(NCH):
                wcr_mc(mc)

        def ck_kk(q, tix):
            wslA = ckw[q][0]

            def kk_fl(fl):
                bnk = pb[fl % 2]

                def mm(kc):
                    E("tensor", lambda e: e.matmul(
                        bnk.h[:, 0:512],
                        wslA.h[:, kc * 1024 + fl * 128:kc * 1024 + (fl + 1) * 128],
                        mxck.h[:, kc * 1024 + tix * 512:kc * 1024 + (tix + 1) * 512],
                        start=(kc == 0), stop=(kc == 7)),
                      reads=[wslA.iv(kc * 1024, (kc + 1) * 1024),
                             mxck.iv(kc * 1024 + tix * 512,
                                     kc * 1024 + (tix + 1) * 512)],
                      writes=piv(bnk))

                for kc in range(NCH):
                    mm(kc)
                rl = rls[fl % 2]
                E("scalar", lambda e: e.activation(
                    rl.h[:, 0:512], bnk.h[:, 0:512], AF.Relu),
                  reads=piv(bnk), writes=[rl.iv()])
                kq = (tix * 8 + fl) * 512
                E("gpsimd", lambda e: e.tensor_tensor(
                    kkq.h[:, kq:kq + 512], rl.h[:, 0:512], rl.h[:, 0:512],
                    OP.mult),
                  reads=[rl.iv()], writes=[kkq.iv(kq, kq + 512)])

            for fl in range(8):
                kk_fl(fl)

        def ck_kv(q, tix):
            wslB = ckw[q][1]

            def kv_mc(mc):
                bnk = pb[2 + mc % 4]

                def mm(fl):
                    kq = (tix * 8 + fl) * 512
                    E("tensor", lambda e: e.matmul(
                        bnk.h[:, 0:512],
                        wslB.h[:, fl * 1024 + mc * 128:fl * 1024 + (mc + 1) * 128],
                        kkq.h[:, kq:kq + 512],
                        start=(fl == 0), stop=(fl == 7)),
                      reads=[wslB.iv(fl * 1024, (fl + 1) * 1024),
                             kkq.iv(kq, kq + 512)],
                      writes=piv(bnk))

                for fl in range(8):
                    mm(fl)
                acc = kvacc[mc]
                if q == 0:
                    E("vector", lambda e: e.tensor_copy(
                        acc.h[:, tix * 512:(tix + 1) * 512], bnk.h[:, 0:512]),
                      reads=piv(bnk), writes=[acc.iv(tix * 512, (tix + 1) * 512)])
                else:
                    E("vector", lambda e: e.tensor_tensor(
                        acc.h[:, tix * 512:(tix + 1) * 512], bnk.h[:, 0:512],
                        acc.h[:, tix * 512:(tix + 1) * 512], OP.add),
                      reads=piv(bnk) + [acc.iv(tix * 512, (tix + 1) * 512)],
                      writes=[acc.iv(tix * 512, (tix + 1) * 512)])

            for mc in range(NCH):
                kv_mc(mc)

        for g0_, g1_ in st["gt"]:
            wo_mm_tile(g0_, g1_)
        rt_emitted = [False] * ntile
        for gix, (g0, g1) in enumerate(st["gt"]):
            wo_tile(gix, g0, g1)
            for t_, (rt0_, rt1_) in enumerate(st["rt"]):
                if not rt_emitted[t_] and rt1_ <= g1:
                    rt_emitted[t_] = True
                    firstC_ = (sti == 1 and t_ == 0)
                    wcr_tile(t_, rt0_)
                    for kc in range(NCH):
                        cmix(mxck, kc * 1024 + t_ * 512, kc, rt0_ - dom,
                             firstC_, "cm_k", dsc3)
                    ck_kk(0, t_)
        if sti == 0:
            tap("x1", XN[0].h[:, 0:512], [XN[0].iv(0, 512)])
            tap("xc0", xcS[0].h[:, 0:512], [xcS[0].iv(0, 512)])
            tap("rr0", xaT[0].h[:, 0:512], [xaT[0].iv(0, 512)])

            def xc_car(kc):
                E("vector", lambda e: e.tensor_copy(
                    xccar.h[:, kc:kc + 1], xcS[kc].h[:, 1151:1152]),
                  reads=[xcS[kc].iv(1151, 1152)], writes=[xccar.iv(kc, kc + 1)])
            for kc in range(NCH):
                xc_car(kc)

        ckw[0][1] = wload(wdram["Wcv"][0])
        for tix0 in range(ntile):
            ck_kv(0, tix0)
        for q in range(1, 4):
            ckw[q] = [wload(wdram["Wck"][q]), wload(wdram["Wcv"][q])]
            for tix0 in range(ntile):
                ck_kk(q, tix0)
            for tix0 in range(ntile):
                ck_kv(q, tix0)

        def s_tile(tix, t0):
            bcol = t0 - dom

            def s_mc(mc):
                E("vector", lambda e: e.tensor_tensor(
                    kvf.h[:, 0:512], kvacc[mc].h[:, tix * 512:(tix + 1) * 512],
                    xaT[mc].h[:, tix * 512:(tix + 1) * 512], OP.mult),
                  reads=[kvacc[mc].iv(tix * 512, (tix + 1) * 512),
                         xaT[mc].iv(tix * 512, (tix + 1) * 512)],
                  writes=[kvf.iv()])
                E("vector", lambda e: e.tensor_tensor(
                    XN[mc].h[:, bcol:bcol + 512], XN[mc].h[:, bcol:bcol + 512],
                    kvf.h[:, 0:512], OP.add),
                  reads=[XN[mc].iv(bcol, bcol + 512), kvf.iv()],
                  writes=[XN[mc].iv(bcol, bcol + 512)])

            for mc in range(NCH):
                s_mc(mc)

        for tix, (t0, t1) in enumerate(st["rt"]):
            s_tile(tix, t0)

        arena.off = 0
        sgF = [arena.take(512, F32) for _ in range(NCH)]
        xrw = [arena.take(1024, F32) for _ in range(6)]
        osb = [arena.take(1024, F32) for _ in range(2)]

        def out_tile(t0, tb):
            bcol = t0 - dom
            gcol = t0 - gdom

            def sg_mc(mc):
                E("vector", lambda e: e.tensor_tensor(
                    sgF[mc].h[:, 0:512], XN[mc].h[:, bcol:bcol + 512],
                    gS[mc].h[:, gcol:gcol + 512], OP.mult),
                  reads=[XN[mc].iv(bcol, bcol + 512),
                         gS[mc].iv(gcol, gcol + 512)],
                  writes=[sgF[mc].iv()])

            for mc in range(NCH):
                sg_mc(mc)

            def load_j(j):
                r0 = t0 + j * 128
                s = (tb + j) % 6
                E("sync", lambda e: e.dma_start(out=xrw[s].h[:],
                                                in_=xsl[r0:r0 + 128, :]),
                  writes=[xrw[s].iv()], dma=f"xr{s}")

            for j in range(4):
                load_j(j)

            def out_j(j):
                r0 = t0 + j * 128
                par = j % 2

                def tr_mc(mc):
                    bnk = pb[mc]
                    E("tensor", lambda e: e.transpose(
                        bnk.h[:, 0:128], sgF[mc].h[:, j * 128:(j + 1) * 128],
                        idn.h[:]),
                      reads=[sgF[mc].iv(j * 128, (j + 1) * 128), idn.iv()],
                      writes=piv(bnk))
                    E("vector", lambda e: e.tensor_tensor(
                        osb[par].h[:, mc * 128:(mc + 1) * 128], bnk.h[:, 0:128],
                        xrw[(tb + j) % 6].h[:, mc * 128:(mc + 1) * 128], OP.add),
                      reads=piv(bnk) + [xrw[(tb + j) % 6].iv()],
                      writes=[osb[par].iv(mc * 128, (mc + 1) * 128)])

                for mc in range(NCH):
                    tr_mc(mc)
                E("sync", lambda e: e.dma_start(
                    out=osl[r0 - HALO:r0 - HALO + 128, :], in_=osb[par].h[:]),
                  reads=[osb[par].iv()], writes=[], dma=f"ost{par}")

            for j in range(4):
                out_j(j)

        for tix0, (t0, t1) in enumerate(st["rt"]):
            out_tile(t0, tix0 * 4)

    for sti_, st_ in enumerate(STS):
        do_st(sti_, st_)

    dkeys = P.finalize()
    guaranteed = {e: {} for e in Prog.ENGS}
    with contextlib.ExitStack() as ctx:
        esems = {e: ctx.enter_context(nc.semaphore("s_" + e)) for e in Prog.ENGS}
        dsems = {k: ctx.enter_context(nc.semaphore("d_" + k)) for k in dkeys}
        blk = ctx.enter_context(nc.Block())

        @blk.tensor
        def _(t):
            P.replay("tensor", t, esems, dsems, guaranteed)

        @blk.vector
        def _(v):
            P.replay("vector", v, esems, dsems, guaranteed)

        @blk.scalar
        def _(s):
            P.replay("scalar", s, esems, dsems, guaranteed)

        @blk.gpsimd
        def _(g):
            P.replay("gpsimd", g, esems, dsems, guaranteed)

        @blk.sync
        def _(sy):
            P.replay("sync", sy, esems, dsems, guaranteed)
            for k in dkeys:
                sy.wait_ge(dsems[k], P.dtot[k] * 16)
    return nc


def _host_maps(inputs):
    x = np.ascontiguousarray(np.asarray(inputs["L0_attn_x"], dtype=np.float32))

    def f32(a):
        return np.asarray(a, dtype=np.float32)

    def sq(w):
        return np.ascontiguousarray(
            f32(w).reshape(8, 128, 1024).transpose(1, 0, 2).reshape(128, 8192))

    shared = {
        "Wk": sq(inputs["L0_attn_Wk"]), "Wv": sq(inputs["L0_attn_Wv"]),
        "Wr": sq(inputs["L0_attn_Wr"]), "Wo": sq(inputs["L0_attn_Wo"]),
        "Wcr": sq(inputs["L0_attn_Wcr"]),
        "Wck": np.ascontiguousarray(f32(inputs["L0_attn_Wck"]).reshape(
            8, 128, 4, 1024).transpose(2, 1, 0, 3).reshape(4, 128, 8192)),
        "Wcv": np.ascontiguousarray(f32(inputs["L0_attn_Wcv"]).reshape(
            4, 8, 128, 1024).transpose(0, 2, 1, 3).reshape(4, 128, 8192)),
        "Wffn": np.ascontiguousarray(f32(inputs["L0_ffn_W"]).reshape(
            8, 8, 128, 1024).transpose(0, 2, 1, 3).reshape(8, 128, 8192)),
        "bffn": inputs["L0_ffn_b"], "sigs": inputs["L0_ffn_tile_sigs"],
        "ln0_g": inputs["L0_attn_ln0_g"], "ln0_b": inputs["L0_attn_ln0_b"],
        "ln1_g": inputs["L0_attn_ln1_g"], "ln1_b": inputs["L0_attn_ln1_b"],
        "time_decay": inputs["L0_attn_time_decay"],
        "time_first": inputs["L0_attn_time_first"],
        "tm_k": inputs["L0_attn_tm_k"], "tm_v": inputs["L0_attn_tm_v"],
        "tm_r": inputs["L0_attn_tm_r"], "cm_k": inputs["L0_attn_cm_k"],
        "cm_r": inputs["L0_attn_cm_r"],
    }
    shared = {k: np.ascontiguousarray(np.asarray(v, dtype=np.float32))
              for k, v in shared.items()}
    in_maps = []
    for core in range(8):
        b = core // 4
        t0 = (core % 4) * REAL
        lo = t0 - HALO
        xs = np.zeros((EXT, C), np.float32)
        msk = np.zeros((1, HALO), np.float32)
        src_lo = max(lo, 0)
        xs[src_lo - lo:] = x[b, src_lo:t0 + REAL]
        msk[0, src_lo - lo:] = 1.0
        msk[0, 0] = 0.0
        lr = ((1.0 - msk) * LOGMASK).astype(np.float32)
        im = dict(shared)
        im["xsl"] = xs
        im["mrow"] = msk
        im["lrow"] = lr
        in_maps.append(im)
    return in_maps


def kernel(**inputs):
    nc = build_nc()
    in_maps = _host_maps(inputs)
    res = run_bass_kernel_spmd(nc, in_maps, core_ids=list(range(8)))
    out = np.zeros((2, 8192, C), np.float32)
    for core in range(8):
        b = core // 4
        t0 = (core % 4) * REAL
        out[b, t0:t0 + REAL] = res.results[core]["osl"]
    return out


def run_debug(inputs, taps):
    global TAPS_WANT
    TAPS_WANT = tuple(taps)
    try:
        nc = build_nc()
        in_maps = _host_maps(inputs)
        res = run_bass_kernel_spmd(nc, in_maps, core_ids=list(range(8)))
        out = np.zeros((2, 8192, C), np.float32)
        for core in range(8):
            b = core // 4
            t0 = (core % 4) * REAL
            out[b, t0:t0 + REAL] = res.results[core]["osl"]
        dbgs = [res.results[core].get("dbg") for core in range(8)]
        return out, dbgs
    finally:
        TAPS_WANT = ()
SERIAL = False
SERIAL_FROM = 0
SERIAL_TO = 10**9
ONLY_LOAD = False
PHASE_MARKS = {}


if __name__ == "__main__":
    import sys
    if "--build" in sys.argv:
        nc_ = build_nc()
        print("built ok; instructions:", nc_.n_instructions())


# version 24 (final)
# speedup vs baseline: 1.1035x; 1.0085x over previous
import contextlib
import numpy as np
import concourse.bass as bass
import concourse.mybir as mybir
from concourse.bass_utils import run_bass_kernel_spmd

F32 = mybir.dt.float32
BF16 = mybir.dt.bfloat16
U8 = mybir.dt.uint8
AF = mybir.ActivationFunctionType
OP = mybir.AluOpType
AX = mybir.AxisListType.X

TAPS_WANT = ()
SERIAL = False
SERIAL_FROM = 0
SERIAL_TO = 10**9
ONLY_LOAD = False
PHASE_MARKS = {}

C = 1024
NCH = 8
HALO = 256
REAL = 2048
EXT = HALO + REAL
EPS = 1e-5
LOGMASK = -30000.0

STS = [
    dict(x0=0, x1=1280, dom=128, gdom=256,
         kvr=[(0, 256), (256, 768), (768, 1280)],
         gt=[(128, 256), (256, 768), (768, 1280)],
         rt=[(256, 768), (768, 1280)]),
    dict(x0=1280, x1=2304, dom=1280, gdom=1280,
         kvr=[(1280, 1792), (1792, 2304)],
         gt=[(1280, 1792), (1792, 2304)],
         rt=[(1280, 1792), (1792, 2304)]),
]


class Buf:
    def __init__(self, handle, track, byte_off, ebytes, cols):
        self.h = handle
        self.track = track
        self.off = byte_off
        self.eb = ebytes
        self.cols = cols

    def iv(self, lo=0, hi=None):
        if hi is None:
            hi = self.cols
        return (self.track, self.off + lo * self.eb, self.off + hi * self.eb)


class Prog:
    ENGS = ("tensor", "vector", "scalar", "gpsimd", "sync")

    def __init__(self):
        self.ops = []
        self.hist = {}

    def emit(self, eng, fn, reads=(), writes=(), dma=None, tag=""):
        i = len(self.ops)
        deps = set()
        for (t, lo, hi) in reads:
            h = self.hist.get(t)
            if h:
                for (l2, h2, j) in h["w"]:
                    if l2 < hi and lo < h2:
                        deps.add(j)
        for (t, lo, hi) in writes:
            h = self.hist.get(t)
            if h:
                lw = -1
                for (l2, h2, j) in h["w"]:
                    if l2 < hi and lo < h2:
                        deps.add(j)
                        lw = max(lw, j)
                for (l2, h2, j) in h["r"]:
                    if l2 < hi and lo < h2 and j > lw:
                        deps.add(j)
        self.ops.append({"eng": eng, "fn": fn, "deps": deps, "dma": dma,
                         "tag": tag, "need_inc": dma is not None})
        for (t, lo, hi) in reads:
            self.hist.setdefault(t, {"w": [], "r": []})["r"].append((lo, hi, i))
        for (t, lo, hi) in writes:
            self.hist.setdefault(t, {"w": [], "r": []})["w"].append((lo, hi, i))
        return i

    def finalize(self):
        ops = self.ops
        if SERIAL:
            for i, op in enumerate(ops):
                if SERIAL_FROM < i < SERIAL_TO:
                    op["deps"].add(i - 1)
        for op in ops:
            for j in op["deps"]:
                p = ops[j]
                if p["eng"] == op["eng"] == "tensor" and p["dma"] is None:
                    continue
                p["need_inc"] = True
        ecnt = {e: 0 for e in self.ENGS}
        dcnt = {}
        for op in ops:
            if op["dma"] is not None:
                k = op["dma"]
                dcnt[k] = dcnt.get(k, 0) + 1
                op["inc_val"] = dcnt[k] * 16
            elif op["need_inc"]:
                ecnt[op["eng"]] += 1
                op["inc_val"] = ecnt[op["eng"]]
        pref = {}
        run = {}
        for i, op in enumerate(ops):
            pref[i] = None
            if op["dma"] is not None:
                run[op["dma"]] = run.get(op["dma"], 0) + 1
                op["kcount"] = run[op["dma"]]
        for i, op in enumerate(ops):
            need = {}
            for j in op["deps"]:
                p = ops[j]
                if p["dma"] is not None:
                    key = ("d", p["dma"])
                    v = 16 * self._emitted_before(ops, p["dma"], i)
                elif p["eng"] == op["eng"] == "tensor":
                    continue
                else:
                    key = ("e", p["eng"])
                    v = p["inc_val"]
                if need.get(key, 0) < v:
                    need[key] = v
            op["waits"] = need
        self.dtot = dcnt
        return sorted(dcnt)

    def _emitted_before(self, ops, key, idx):
        cache = getattr(self, "_ebcache", None)
        if cache is None or cache[0] is not ops:
            pc = {}
            counts = {}
            for i, op in enumerate(ops):
                pc[i] = dict(counts) if op["deps"] else None
                if op["dma"] is not None:
                    counts[op["dma"]] = counts.get(op["dma"], 0) + 1
            self._ebcache = (ops, pc, counts)
        pc = self._ebcache[1][idx]
        if pc is None:
            pc = {}
        return pc.get(key, 0)

    def replay(self, eng_name, eng_obj, esems, dsems, guaranteed):
        for op in self.ops:
            if op["eng"] != eng_name:
                continue
            g = guaranteed[eng_name]
            for (kind, key), v in sorted(op["waits"].items()):
                if g.get((kind, key), 0) >= v:
                    continue
                eng_obj.wait_ge(dsems[key] if kind == "d" else esems[key], v)
                g[(kind, key)] = v
            inst = op["fn"](eng_obj)
            if op["dma"] is not None:
                inst.then_inc(dsems[op["dma"]], 16)
            elif op["need_inc"]:
                inst.then_inc(esems[eng_name], 1)


def build_nc():
    nc = bass.Bass("TRN2", target_bir_lowering=False)

    xsl = nc.dram_tensor("xsl", [EXT, C], F32, kind="ExternalInput").ap()
    mrow = nc.dram_tensor("mrow", [1, HALO], F32, kind="ExternalInput").ap()
    lrow = nc.dram_tensor("lrow", [1, HALO], F32, kind="ExternalInput").ap()
    wdram = {}
    for nm in ["Wk", "Wv", "Wr", "Wo", "Wcr"]:
        wdram[nm] = nc.dram_tensor(nm, [128, 8192], F32,
                                   kind="ExternalInput").ap()
    for nm in ["Wck", "Wcv"]:
        wdram[nm] = nc.dram_tensor(nm, [4, 128, 8192], F32,
                                   kind="ExternalInput").ap()
    wffn = nc.dram_tensor("Wffn", [8, 128, 8192], F32,
                          kind="ExternalInput").ap()
    bffn = nc.dram_tensor("bffn", [8, C], F32, kind="ExternalInput").ap()
    sigs = nc.dram_tensor("sigs", [8, C], F32, kind="ExternalInput").ap()
    vec_in = {}
    for nm in ["ln0_g", "ln0_b", "ln1_g", "ln1_b", "time_decay", "time_first",
               "tm_k", "tm_v", "tm_r", "cm_k", "cm_r"]:
        vec_in[nm] = nc.dram_tensor(nm, [C], F32, kind="ExternalInput").ap()
    osl = nc.dram_tensor("osl", [REAL, C], F32, kind="ExternalOutput").ap()

    def alloc(name, shape, dt):
        h = nc.alloc_sbuf_tensor(name, shape, dt)
        eb = {BF16: 2, U8: 1}.get(dt, 4)
        return Buf(h, name, 0, eb, shape[-1])

    xaT = [alloc(f"xaT{k}", [128, 1280], BF16) for k in range(NCH)]
    XN = [alloc(f"XN{k}", [128, 1152], BF16) for k in range(NCH)]
    rwk = [alloc(f"rwk{k}", [128, 1152], BF16) for k in range(NCH)]
    xcS = [alloc(f"xcS{k}", [128, 1152], BF16) for k in range(NCH)]
    gS = [alloc(f"gS{k}", [128, 1024], BF16) for k in range(NCH)]
    VN = ["ln0_g", "ln0_b", "ln1_g", "ln1_b", "tm_k", "tm_v", "tm_r",
          "cm_k", "cm_r", "time_decay", "time_first", "ew", "eu"]
    vecs = {nm: alloc("v_" + nm, [128, 8], F32) for nm in VN}
    sgnT = alloc("sgnT", [128, 64], BF16)
    idn = alloc("idn", [128, 128], F32)
    m01b = alloc("m01b", [128, HALO], BF16)
    m01u = alloc("m01u", [128, HALO], U8)
    lmB = alloc("lmB", [128, HALO], F32)
    bffn_sb = alloc("bffn_sb", [8, C], BF16)
    selb = alloc("selb", [8, 1024], BF16)
    ones_f = alloc("ones_f", [1, 128], F32)
    onescol = alloc("onescol", [128, 1], F32)
    onescol_b = alloc("onescol_b", [128, 1], BF16)
    epscol = alloc("epscol", [128, 1], F32)
    tinycol = alloc("tinycol", [128, 1], F32)
    zerot = alloc("zerot", [128, 128], BF16)
    Ast = alloc("Ast", [128, 8], F32)
    Bst = alloc("Bst", [128, 8], F32)
    xacar = alloc("xacar", [128, 8], BF16)
    xccar = alloc("xccar", [128, 8], BF16)

    AR_F32 = 15744
    ar_h = nc.alloc_sbuf_tensor("arena", [128, AR_F32], F32)
    ar16 = ar_h.bitcast(BF16)
    ar8 = ar_h.bitcast(U8)
    NSLOT = 3
    WPOOL = [alloc(f"wp{i}", [128, 8192], BF16) for i in range(NSLOT)]

    class Arena:
        def __init__(self):
            self.off = 0

        def take(self, cols, dt):
            eb = {BF16: 2, U8: 1}.get(dt, 4)
            self.off = (self.off + 63) & ~63
            o = self.off
            self.off += cols * eb
            assert self.off <= AR_F32 * 4, f"arena overflow {self.off}"
            if dt == F32:
                h = ar_h[:, o // 4:o // 4 + cols]
            elif dt == U8:
                h = ar8[:, o:o + cols]
            else:
                h = ar16[:, o // 2:o // 2 + cols]
            return Buf(h, "ARENA", o, eb, cols)

    arena = Arena()
    pb = [Buf(nc.alloc_psum_tensor(f"pbank{i}", [128, 512], F32),
              f"pbank{i}", 0, 4, 512) for i in range(8)]

    P = Prog()
    E = P.emit

    def piv(b):
        return [b.iv(0, 512)]

    tw = list(TAPS_WANT)
    dbg = None
    if tw:
        dbg = nc.dram_tensor("dbg", [128, 516 * len(tw)], F32,
                             kind="ExternalOutput").ap()

    def tap(tag, ap, ivs):
        if tag not in tw:
            return
        i = tw.index(tag)
        p_, w_ = ap.shape[0], ap.shape[1]
        dst = dbg[0:p_, i * 516:i * 516 + w_]
        E("gpsimd", (lambda a_, d_: lambda e: e.dma_start(out=d_, in_=a_))(ap, dst),
          reads=list(ivs), writes=[], dma="tapq")

    it_a = arena.take(128, F32)
    it_b = arena.take(128, F32)
    vstage = arena.take(128, F32)
    sstage = arena.take(1024, F32)
    mstage = arena.take(HALO, F32)
    spi = arena.take(1024, F32)
    m01f = arena.take(HALO, F32)

    E("gpsimd", lambda e: e.iota(it_a.h, pattern=[[0, 128]], channel_multiplier=1,
                                 allow_small_or_imprecise_dtypes=True),
      writes=[it_a.iv()])
    E("gpsimd", lambda e: e.iota(it_b.h, pattern=[[1, 128]], channel_multiplier=0,
                                 allow_small_or_imprecise_dtypes=True),
      writes=[it_b.iv()])
    E("vector", lambda e: e.tensor_tensor(idn.h[:], it_a.h, it_b.h, OP.is_equal),
      reads=[it_a.iv(), it_b.iv()], writes=[idn.iv()])
    for pair in [(ones_f, 1.0), (onescol, 1.0), (onescol_b, 1.0), (zerot, 0.0),
                 (Ast, 0.0), (Bst, 0.0), (epscol, EPS), (tinycol, 1e-30)]:
        E("gpsimd", (lambda bb, vv: lambda e: e.memset(bb.h[:], vv))(*pair),
          writes=[pair[0].iv()])

    def init_vec(nm):
        dst = vecs[nm]
        src = vec_in[nm].rearrange("(c p) -> c p", p=128)
        E("sync", lambda e: e.dma_start(out=vstage.h[0:8, 0:128], in_=src),
          writes=[vstage.iv()], dma="vst")
        E("tensor", lambda e: e.transpose(pb[0].h[0:128, 0:8],
                                          vstage.h[0:8, 0:128],
                                          idn.h[0:8, 0:8]),
          reads=[vstage.iv(), idn.iv()], writes=piv(pb[0]))
        E("vector", lambda e: e.tensor_copy(dst.h[:], pb[0].h[0:128, 0:8]),
          reads=piv(pb[0]), writes=[dst.iv()])

    for nm in ["ln0_g", "ln0_b", "ln1_g", "ln1_b", "tm_k", "tm_v", "tm_r",
               "cm_k", "cm_r", "time_decay", "time_first"]:
        init_vec(nm)
    td, tf = vecs["time_decay"], vecs["time_first"]
    E("scalar", lambda e: e.activation(td.h[:], td.h[:], AF.Exp),
      reads=[td.iv()], writes=[td.iv()])
    E("scalar", lambda e: e.activation(vecs["ew"].h[:], td.h[:], AF.Exp, scale=-1.0),
      reads=[td.iv()], writes=[vecs["ew"].iv()])
    E("scalar", lambda e: e.activation(vecs["eu"].h[:], tf.h[:], AF.Exp),
      reads=[tf.iv()], writes=[vecs["eu"].iv()])

    E("gpsimd", lambda e: e.iota(spi.h[0:8, 0:1024], pattern=[[0, 1024]],
                                 channel_multiplier=1,
                                 allow_small_or_imprecise_dtypes=True),
      writes=[spi.iv()])
    E("gpsimd", lambda e: e.iota(sstage.h[0:8, 0:1024], pattern=[[1, 8], [0, 128]],
                                 channel_multiplier=0,
                                 allow_small_or_imprecise_dtypes=True),
      writes=[sstage.iv()])
    E("vector", lambda e: e.tensor_tensor(selb.h[:], spi.h[0:8, 0:1024],
                                          sstage.h[0:8, 0:1024], OP.is_equal),
      reads=[spi.iv(), sstage.iv()], writes=[selb.iv()])

    E("sync", lambda e: e.dma_start(out=sstage.h[0:8, 0:1024], in_=sigs),
      writes=[sstage.iv()], dma="sgq")
    E("scalar", lambda e: e.activation(sstage.h[0:8, 0:1024],
                                       sstage.h[0:8, 0:1024], AF.Sign),
      reads=[sstage.iv()], writes=[sstage.iv()])

    def init_sgn(kc):
        E("tensor", lambda e: e.transpose(pb[1].h[0:128, 0:8],
                                          sstage.h[0:8, kc * 128:(kc + 1) * 128],
                                          idn.h[0:8, 0:8]),
          reads=[sstage.iv(), idn.iv()], writes=piv(pb[1]))
        E("vector", lambda e: e.tensor_copy(sgnT.h[:, kc * 8:(kc + 1) * 8],
                                            pb[1].h[0:128, 0:8]),
          reads=piv(pb[1]), writes=[sgnT.iv(kc * 8, kc * 8 + 8)])

    for kc in range(NCH):
        init_sgn(kc)

    E("gpsimd", lambda e: e.dma_start(out=bffn_sb.h[:], in_=bffn),
      writes=[bffn_sb.iv()], dma="bffn")

    E("sync", lambda e: e.dma_start(out=mstage.h[0:1, 0:HALO], in_=mrow),
      writes=[mstage.iv()], dma="msq")
    E("tensor", lambda e: e.matmul(pb[2].h[:, 0:HALO], ones_f.h[:],
                                   mstage.h[0:1, 0:HALO], start=True, stop=True),
      reads=[mstage.iv(), ones_f.iv()], writes=piv(pb[2]))
    E("vector", lambda e: e.tensor_copy(m01f.h[:], pb[2].h[:, 0:HALO]),
      reads=piv(pb[2]), writes=[m01f.iv()])
    E("sync", lambda e: e.dma_start(out=mstage.h[0:1, 0:HALO], in_=lrow),
      writes=[mstage.iv()], dma="msq")
    E("tensor", lambda e: e.matmul(pb[3].h[:, 0:HALO], ones_f.h[:],
                                   mstage.h[0:1, 0:HALO], start=True, stop=True),
      reads=[mstage.iv(), ones_f.iv()], writes=piv(pb[3]))
    E("vector", lambda e: e.tensor_copy(lmB.h[:], pb[3].h[:, 0:HALO]),
      reads=piv(pb[3]), writes=[lmB.iv()])
    E("vector", lambda e: e.tensor_copy(m01b.h[:], m01f.h[:]),
      reads=[m01f.iv()], writes=[m01b.iv()])
    E("vector", lambda e: e.tensor_copy(m01u.h[:], m01f.h[:]),
      reads=[m01f.iv()], writes=[m01u.iv()])

    wrot = [0]

    def wload(src):
        i = wrot[0] % NSLOT
        wrot[0] += 1
        s = WPOOL[i]
        E("gpsimd", (lambda ss, sap: lambda e: e.dma_start(out=ss.h[:, 0:8192],
                                                           in_=sap))(s, src),
          writes=[s.iv(0, 8192)], dma=f"wp{i}")
        return s

    global PHASE_MARKS
    PHASE_MARKS = {}

    def mark(name):
        PHASE_MARKS[name] = len(P.ops)

    def do_st(sti, st):
        x0, dom, gdom = st["x0"], st["dom"], st["gdom"]
        nx = st["x1"] - x0

        arena.off = 0
        xt = [arena.take(1024, F32) for _ in range(6)]
        sql = [arena.take(1024, F32) for _ in range(3)]
        xnt = [arena.take(1024, F32) for _ in range(3)]
        stc = [[arena.take(1, F32) for _ in range(5)] for _ in range(3)]

        def load_dma(ti):
            r0 = x0 + ti * 128
            xb = xt[ti % 6]
            E("sync", lambda e: e.dma_start(out=xb.h[:], in_=xsl[r0:r0 + 128, :]),
              writes=[xb.iv()], dma=f"xt{ti % 6}")

        def load_s1(ti):
            par = ti % 3
            xb, sq = xt[ti % 6], sql[par]
            ssq, m, v, rst, nmr = stc[par]
            E("scalar", lambda e: e.activation(out=sq.h, in_=xb.h, func=AF.Square,
                                               accum_out=ssq.h[:]),
              reads=[xb.iv()], writes=[sq.iv(), ssq.iv()])
            E("vector", lambda e: e.reduce_sum(m.h, xb.h, axis=AX),
              reads=[xb.iv()], writes=[m.iv()])
            E("vector", lambda e: e.tensor_scalar(m.h, m.h, 1.0 / C, None,
                                                  op0=OP.mult),
              reads=[m.iv()], writes=[m.iv()])
            E("vector", lambda e: e.tensor_tensor(nmr.h, m.h, m.h, OP.mult),
              reads=[m.iv()], writes=[nmr.iv()])
            E("vector", lambda e: e.scalar_tensor_tensor(
                v.h, ssq.h, 1.0 / C, nmr.h, op0=OP.mult, op1=OP.subtract),
              reads=[ssq.iv(), nmr.iv()], writes=[v.iv()])
            E("scalar", lambda e: e.activation(v.h, v.h, AF.Ln, bias=epscol.h[:]),
              reads=[v.iv(), epscol.iv()], writes=[v.iv()])
            E("scalar", lambda e: e.activation(rst.h, v.h, AF.Exp, scale=-0.5),
              reads=[v.iv()], writes=[rst.iv()])

        def load_s2(ti):
            par = ti % 3
            xb, sq, xh = xt[ti % 6], sql[par], xnt[par]
            ssq, m, v, rst, nmr = stc[par]
            E("vector", lambda e: e.scalar_tensor_tensor(
                nmr.h, m.h, -1.0, rst.h, op0=OP.mult, op1=OP.mult),
              reads=[m.iv(), rst.iv()], writes=[nmr.iv()])
            E("vector", lambda e: e.tensor_scalar(
                xh.h, xb.h, rst.h[:], nmr.h[:], op0=OP.mult, op1=OP.add),
              reads=[xb.iv(), rst.iv(), nmr.iv()], writes=[xh.iv()])
            E("scalar", lambda e: e.activation(out=sq.h, in_=xh.h, func=AF.Square,
                                               accum_out=ssq.h[:]),
              reads=[xh.iv()], writes=[sq.iv(), ssq.iv()])
            E("vector", lambda e: e.tensor_scalar(v.h, ssq.h, 1.0 / C, None,
                                                  op0=OP.mult),
              reads=[ssq.iv()], writes=[v.iv()])
            E("scalar", lambda e: e.activation(v.h, v.h, AF.Ln, bias=epscol.h[:]),
              reads=[v.iv(), epscol.iv()], writes=[v.iv()])
            E("scalar", lambda e: e.activation(rst.h, v.h, AF.Exp, scale=-0.5),
              reads=[v.iv()], writes=[rst.iv()])

        def load_tile(ti):
            r0 = x0 + ti * 128
            par = ti % 3
            sq, xh = sql[par], xnt[par]
            ssq, m, v, rst, nmr = stc[par]

            def tile_kc(kc):
                if r0 >= dom:
                    pbx = pb[kc % 2]
                    E("tensor", lambda e: e.transpose(
                        pbx.h[:, 0:128], xh.h[:, kc * 128:(kc + 1) * 128],
                        idn.h[:]),
                      reads=[xh.iv(), idn.iv()], writes=piv(pbx))
                    E("vector", lambda e: e.tensor_copy(
                        XN[kc].h[:, r0 - dom:r0 - dom + 128], pbx.h[:, 0:128]),
                      reads=piv(pbx),
                      writes=[XN[kc].iv(r0 - dom, r0 - dom + 128)])
                pby = pb[4 + kc % 2]
                E("scalar", lambda e: e.activation(
                    sq.h[:, kc * 128:(kc + 1) * 128],
                    xh.h[:, kc * 128:(kc + 1) * 128], AF.Copy, scale=rst.h[:]),
                  reads=[xh.iv(), rst.iv()],
                  writes=[sq.iv(kc * 128, kc * 128 + 128)])
                E("tensor", lambda e: e.transpose(
                    pby.h[:, 0:128], sq.h[:, kc * 128:(kc + 1) * 128], idn.h[:]),
                  reads=[sq.iv(kc * 128, kc * 128 + 128), idn.iv()],
                  writes=piv(pby))
                E("vector", lambda e: e.tensor_scalar(
                    xaT[kc].h[:, r0 - x0:r0 - x0 + 128], pby.h[:, 0:128],
                    vecs["ln0_g"].h[:, kc:kc + 1],
                    vecs["ln0_b"].h[:, kc:kc + 1], op0=OP.mult, op1=OP.add),
                  reads=piv(pby) + [vecs["ln0_g"].iv(), vecs["ln0_b"].iv()],
                  writes=[xaT[kc].iv(r0 - x0, r0 - x0 + 128)])
                if sti == 0 and r0 < HALO:
                    E("vector", lambda e: e.tensor_tensor(
                        xaT[kc].h[:, r0 - x0:r0 - x0 + 128],
                        xaT[kc].h[:, r0 - x0:r0 - x0 + 128],
                        m01b.h[:, r0:r0 + 128], OP.mult),
                      reads=[xaT[kc].iv(r0 - x0, r0 - x0 + 128), m01b.iv()],
                      writes=[xaT[kc].iv(r0 - x0, r0 - x0 + 128)])

            for kc in range(NCH):
                tile_kc(kc)

        NT = nx // 128
        for ti in range(min(5, NT)):
            load_dma(ti)
        for w in range(NT + 2):
            if w < NT:
                load_s1(w)
            if 0 <= w - 1 < NT:
                load_s2(w - 1)
            if 0 <= w - 2 < NT:
                load_tile(w - 2)
            if w + 5 < NT:
                load_dma(w + 5)
        if sti == 0:
            tap("xn0", XN[0].h[:, 0:512], [XN[0].iv(0, 512)])
            tap("xa0", xaT[0].h[:, 0:512], [xaT[0].iv(0, 512)])
        if sti == 0:
            def xa_car(kc):
                E("vector", lambda e: e.tensor_copy(
                    xacar.h[:, kc:kc + 1], xaT[kc].h[:, 1279:1280]),
                  reads=[xaT[kc].iv(1279, 1280)], writes=[xacar.iv(kc, kc + 1)])
            for kc in range(NCH):
                xa_car(kc)

        if ONLY_LOAD:
            return
        mark(f"ffn{sti}")
        arena.off = 0
        outac = [Buf(rwk[mc].h[:, 0:1024], f"rwk{mc}", 0, 2, 1024)
                 for mc in range(NCH)]
        M8t = [arena.take(512, BF16) for _ in range(2)]
        mbu = arena.take(16 * 512, U8)
        scsb = arena.take(512, F32)
        scT = arena.take(8, F32)
        mxc = arena.take(1, F32)
        mTf = arena.take(8, F32)
        sil = [arena.take(512, F32) for _ in range(2)]
        fb = st["rt"][0][0] - dom
        ntile = len(st["rt"])

        def ffn_scores(tix, t0, t1):
            L = t1 - t0
            xb = fb + tix * 512
            M8 = M8t[tix]

            def scores_kc(kc):
                E("tensor", lambda e: e.matmul(
                    pb[7].h[0:8, 0:L], sgnT.h[:, kc * 8:(kc + 1) * 8],
                    XN[kc].h[:, xb:xb + L],
                    start=(kc == 0), stop=(kc == 7)),
                  reads=[sgnT.iv(kc * 8, kc * 8 + 8), XN[kc].iv(xb, xb + L)],
                  writes=piv(pb[7]))

            for kc in range(NCH):
                scores_kc(kc)
            E("vector", lambda e: e.tensor_copy(scsb.h[0:8, 0:L],
                                                pb[7].h[0:8, 0:L]),
              reads=piv(pb[7]), writes=[scsb.iv()])

            def mask_j(j):
                E("tensor", lambda e: e.transpose(
                    pb[6].h[0:128, 0:8], scsb.h[0:8, j * 128:(j + 1) * 128],
                    idn.h[0:8, 0:8]),
                  reads=[scsb.iv(), idn.iv()], writes=piv(pb[6]))
                E("vector", lambda e: e.tensor_copy(scT.h[0:128, 0:8],
                                                    pb[6].h[0:128, 0:8]),
                  reads=piv(pb[6]), writes=[scT.iv()])
                E("vector", lambda e: e.reduce_max(mxc.h, scT.h[0:128, 0:8],
                                                   axis=AX),
                  reads=[scT.iv()], writes=[mxc.iv()])
                E("vector", lambda e: e.tensor_scalar(
                    mTf.h[0:128, 0:8], scT.h[0:128, 0:8], mxc.h[:], None,
                    op0=OP.is_ge),
                  reads=[scT.iv(), mxc.iv()], writes=[mTf.iv()])
                E("tensor", lambda e: e.transpose(
                    pb[5].h[0:8, 0:128], mTf.h[0:128, 0:8], idn.h[:]),
                  reads=[mTf.iv(), idn.iv()], writes=piv(pb[5]))
                E("vector", lambda e: e.tensor_copy(
                    M8.h[0:8, j * 128:(j + 1) * 128], pb[5].h[0:8, 0:128]),
                  reads=piv(pb[5]), writes=[M8.iv(j * 128, (j + 1) * 128)])

            for j in range(L // 128):
                mask_j(j)

            def pred_e(e2):
                bnk = pb[(e2 * 2 + tix) % 4]
                off = (e2 * 2 + tix) * 512
                E("tensor", lambda e: e.matmul(
                    bnk.h[:, 0:L], selb.h[0:8, e2 * 128:(e2 + 1) * 128],
                    M8.h[0:8, 0:L], start=True, stop=True),
                  reads=[M8.iv(), selb.iv()], writes=piv(bnk))
                E("vector", lambda e: e.tensor_copy(
                    mbu.h[:, off:off + L], bnk.h[:, 0:L]),
                  reads=piv(bnk), writes=[mbu.iv(off, off + L)])

            for e2 in range(8):
                pred_e(e2)

        for tix0, (t0, t1) in enumerate(st["rt"]):
            ffn_scores(tix0, t0, t1)

        def expert(e2):
            ws = wload(wffn[e2])

            def etile(tix):
                xb = fb + tix * 512

                def mm(mc, kc):
                    E("tensor", lambda e: e.matmul(
                        pb[mc].h[:, 0:512],
                        ws.h[:, kc * 1024 + mc * 128:kc * 1024 + (mc + 1) * 128],
                        XN[kc].h[:, xb:xb + 512],
                        start=(kc == 0), stop=(kc == 7)),
                      reads=[ws.iv(kc * 1024, (kc + 1) * 1024),
                             XN[kc].iv(xb, xb + 512)],
                      writes=piv(pb[mc]))

                for mc in range(NCH):
                    for kc in range(NCH):
                        mm(mc, kc)

                def evac(mc):
                    off = (e2 * 2 + tix) * 512
                    E("vector", lambda e: e.copy_predicated(
                        outac[mc].h[:, tix * 512:(tix + 1) * 512],
                        mbu.h[:, off:off + 512], pb[mc].h[:, 0:512]),
                      reads=piv(pb[mc]) + [mbu.iv(off, off + 512),
                                           outac[mc].iv(tix * 512,
                                                        (tix + 1) * 512)],
                      writes=[outac[mc].iv(tix * 512, (tix + 1) * 512)])

                for mc in range(NCH):
                    evac(mc)

            for tix in range(ntile):
                etile(tix)

        for e2 in reversed(range(8)):
            expert(e2)

        def bias_silu(tix, mc):
            t0 = st["rt"][tix][0]
            gcol = t0 - gdom
            s = sil[mc % 2]
            E("tensor", lambda e: e.matmul(
                pb[mc].h[:, 0:512], bffn_sb.h[0:8, mc * 128:(mc + 1) * 128],
                M8t[tix].h[0:8, 0:512], start=True, stop=True),
              reads=[bffn_sb.iv(), M8t[tix].iv()], writes=piv(pb[mc]))
            E("vector", lambda e: e.tensor_tensor(
                s.h[:, 0:512], outac[mc].h[:, tix * 512:(tix + 1) * 512],
                pb[mc].h[:, 0:512], OP.add),
              reads=[outac[mc].iv(tix * 512, (tix + 1) * 512)] + piv(pb[mc]),
              writes=[s.iv()])
            E("scalar", lambda e: e.activation(
                gS[mc].h[:, gcol:gcol + 512], s.h[:, 0:512], AF.Silu),
              reads=[s.iv()], writes=[gS[mc].iv(gcol, gcol + 512)])

        for tix0 in range(ntile):
            for mc in range(NCH):
                bias_silu(tix0, mc)
        if sti == 0:
            tap("g0", gS[0].h[:, 0:512], [gS[0].iv(0, 512)])
            tap("m8", M8t[0].h[0:8, 0:512], [M8t[0].iv(0, 512)])

        mark(f"kvr{sti}")
        arena.off = 0
        mx_ = {q: [arena.take(8 * 512, BF16) for _ in range(2)] for q in "kvr"}
        dsc = arena.take(512, BF16)
        rwt = Buf(dsc.h[:, 0:256], "ARENA", dsc.off, 2, 256)
        ekb = arena.take(512, F32)
        ekvb = arena.take(512, F32)
        numb = arena.take(512, F32)
        exrb = arena.take(512, F32)
        Abuf = arena.take(513, F32)
        Bbuf = arena.take(513, F32)
        wkS = {"k": wload(wdram["Wk"]), "v": wload(wdram["Wv"]),
               "r": wload(wdram["Wr"])}
        tmv = {"k": "tm_k", "v": "tm_v", "r": "tm_r"}
        if sti == 0:
            def mset(q, kc):
                E("gpsimd", lambda e: e.memset(
                    mx_[q][0].h[:, kc * 512:kc * 512 + 1], 0.0),
                  writes=[mx_[q][0].iv(kc * 512, kc * 512 + 1)])
            for q in "kvr":
                for kc in range(NCH):
                    mset(q, kc)

        def kvr_mixes(tix, t0, t1):
            L = t1 - t0
            lc0 = t0 - x0
            first0 = (sti == 0 and tix == 0)
            firstC = (sti == 1 and tix == 0)
            s0 = 1 if (first0 or firstC) else 0
            pp = tix % 2

            def mix_kc(kc):
                cur = xaT[kc].h[:, lc0 + s0:lc0 + L]
                shf = xaT[kc].h[:, lc0 + s0 - 1:lc0 + L - 1]
                riv = [xaT[kc].iv(max(lc0 - 1, 0), lc0 + L)]
                E("gpsimd", lambda e: e.tensor_tensor(
                    dsc.h[:, 0:L - s0], cur, shf, OP.subtract),
                  reads=riv, writes=[dsc.iv()])

                def mix_q(q):
                    E("vector", lambda e: e.scalar_tensor_tensor(
                        mx_[q][pp].h[:, kc * 512 + s0:kc * 512 + L],
                        dsc.h[:, 0:L - s0], vecs[tmv[q]].h[:, kc:kc + 1], shf,
                        op0=OP.mult, op1=OP.add),
                      reads=[dsc.iv(), vecs[tmv[q]].iv()] + riv,
                      writes=[mx_[q][pp].iv(kc * 512 + s0, kc * 512 + L)])

                for q in "kvr":
                    mix_q(q)
                if firstC:
                    E("vector", lambda e: e.tensor_tensor(
                        dsc.h[:, 0:1], xaT[kc].h[:, 0:1],
                        xacar.h[:, kc:kc + 1], OP.subtract),
                      reads=[xaT[kc].iv(0, 1), xacar.iv(kc, kc + 1)],
                      writes=[dsc.iv(0, 2)])

                    def mix0_q(q):
                        E("vector", lambda e: e.scalar_tensor_tensor(
                            mx_[q][pp].h[:, kc * 512:kc * 512 + 1],
                            dsc.h[:, 0:1], vecs[tmv[q]].h[:, kc:kc + 1],
                            xacar.h[:, kc:kc + 1], op0=OP.mult, op1=OP.add),
                          reads=[dsc.iv(0, 2), vecs[tmv[q]].iv(),
                                 xacar.iv(kc, kc + 1)],
                          writes=[mx_[q][pp].iv(kc * 512, kc * 512 + 1)])

                    for q in "kvr":
                        mix0_q(q)

            for kc in range(NCH):
                mix_kc(kc)

        def kvr_tile(tix, t0, t1):
            L = t1 - t0
            first0 = (sti == 0 and tix == 0)
            pp = tix % 2

            def kvr_mc(mc):
                par = mc % 2
                bk, bv, br = pb[par * 3], pb[par * 3 + 1], pb[par * 3 + 2]

                def proj(q, bnk, kc):
                    E("tensor", lambda e: e.matmul(
                        bnk.h[:, 0:L],
                        wkS[q].h[:, kc * 1024 + mc * 128:kc * 1024 + (mc + 1) * 128],
                        mx_[q][pp].h[:, kc * 512:kc * 512 + L],
                        start=(kc == 0), stop=(kc == 7)),
                      reads=[wkS[q].iv(kc * 1024, (kc + 1) * 1024),
                             mx_[q][pp].iv(kc * 512, kc * 512 + L)],
                      writes=piv(bnk))

                for q, bnk in [("k", bk), ("v", bv), ("r", br)]:
                    for kc in range(NCH):
                        proj(q, bnk, kc)
                if first0:
                    E("vector", lambda e: e.scalar_tensor_tensor(
                        ekb.h[:, 0:L], bk.h[:, 0:L], 1.0, lmB.h[:, 0:L],
                        op0=OP.mult, op1=OP.add),
                      reads=piv(bk) + [lmB.iv()], writes=[ekb.iv()])
                    E("scalar", lambda e: e.activation(ekb.h[:, 0:L],
                                                       ekb.h[:, 0:L], AF.Exp),
                      reads=[ekb.iv()], writes=[ekb.iv()])
                else:
                    E("scalar", lambda e: e.activation(ekb.h[:, 0:L],
                                                       bk.h[:, 0:L], AF.Exp),
                      reads=piv(bk), writes=[ekb.iv()])
                E("vector", lambda e: e.tensor_tensor(
                    ekvb.h[:, 0:L], bv.h[:, 0:L], ekb.h[:, 0:L], OP.mult),
                  reads=piv(bv) + [ekb.iv()], writes=[ekvb.iv()])
                E("scalar", lambda e: e.activation(exrb.h[:, 0:L], br.h[:, 0:L],
                                                   AF.Sigmoid),
                  reads=piv(br), writes=[exrb.iv()])

                def scan(buf, data, stv):
                    E("vector", lambda e: e.tensor_copy(buf.h[:, 0:1],
                                                        stv.h[:, mc:mc + 1]),
                      reads=[stv.iv(mc, mc + 1)], writes=[buf.iv(0, 1)])
                    E("vector", lambda e: e.tensor_tensor_scan(
                        buf.h[:, 1:L + 1],
                        vecs["ew"].h[:, mc:mc + 1].broadcast_to([128, L]),
                        data.h[:, 0:L], initial=stv.h[:, mc:mc + 1],
                        op0=OP.mult, op1=OP.add),
                      reads=[data.iv(), stv.iv(mc, mc + 1), vecs["ew"].iv()],
                      writes=[buf.iv(1, L + 1)])
                    E("vector", lambda e: e.tensor_copy(stv.h[:, mc:mc + 1],
                                                        buf.h[:, L:L + 1]),
                      reads=[buf.iv(L, L + 1)], writes=[stv.iv(mc, mc + 1)])

                if sti == 0 and tix == 1 and mc == 0:
                    tap("ek", ekb.h[:, 0:512], [ekb.iv()])
                    tap("ekv", ekvb.h[:, 0:512], [ekvb.iv()])
                scan(Abuf, ekvb, Ast)
                scan(Bbuf, ekb, Bst)
                if sti == 0 and tix == 1 and mc == 0:
                    tap("Abuf", Abuf.h[:, 0:513], [Abuf.iv()])
                E("vector", lambda e: e.scalar_tensor_tensor(
                    numb.h[:, 0:L], ekvb.h[:, 0:L],
                    vecs["eu"].h[:, mc:mc + 1], Abuf.h[:, 0:L],
                    op0=OP.mult, op1=OP.add),
                  reads=[ekvb.iv(), vecs["eu"].iv(), Abuf.iv(0, L)],
                  writes=[numb.iv()])
                E("vector", lambda e: e.scalar_tensor_tensor(
                    ekb.h[:, 0:L], ekb.h[:, 0:L],
                    vecs["eu"].h[:, mc:mc + 1], Bbuf.h[:, 0:L],
                    op0=OP.mult, op1=OP.add),
                  reads=[ekb.iv(), vecs["eu"].iv(), Bbuf.iv(0, L)],
                  writes=[ekb.iv()])
                E("scalar", lambda e: e.activation(ekvb.h[:, 0:L],
                                                   ekb.h[:, 0:L], AF.Ln,
                                                   bias=tinycol.h[:]),
                  reads=[ekb.iv(), tinycol.iv()], writes=[ekvb.iv()])
                E("scalar", lambda e: e.activation(ekvb.h[:, 0:L],
                                                   ekvb.h[:, 0:L], AF.Exp,
                                                   scale=-1.0),
                  reads=[ekvb.iv()], writes=[ekvb.iv()])
                E("vector", lambda e: e.tensor_tensor(
                    numb.h[:, 0:L], numb.h[:, 0:L], ekvb.h[:, 0:L], OP.mult),
                  reads=[numb.iv(), ekvb.iv()], writes=[numb.iv()])
                ds = max(t0, dom) - dom
                so = max(t0, dom) - t0
                n = t1 - max(t0, dom)
                if first0:
                    E("vector", lambda e: e.tensor_tensor(
                        rwt.h, numb.h[:, 0:L], exrb.h[:, 0:L], OP.mult),
                      reads=[numb.iv(), exrb.iv()], writes=[rwt.iv()])
                    E("vector", lambda e: e.tensor_copy(
                        rwk[mc].h[:, ds:ds + n], zerot.h[:, 0:n]),
                      reads=[zerot.iv()], writes=[rwk[mc].iv(ds, ds + n)])
                    E("vector", lambda e: e.copy_predicated(
                        rwk[mc].h[:, ds:ds + n], m01u.h[:, so:so + n],
                        rwt.h[:, so:so + n]),
                      reads=[rwt.iv(), m01u.iv(), rwk[mc].iv(ds, ds + n)],
                      writes=[rwk[mc].iv(ds, ds + n)])
                else:
                    E("gpsimd", lambda e: e.tensor_tensor(
                        rwk[mc].h[:, ds:ds + n], numb.h[:, 0:n],
                        exrb.h[:, 0:n], OP.mult),
                      reads=[numb.iv(), exrb.iv()],
                      writes=[rwk[mc].iv(ds, ds + n)])

            for mc in range(NCH):
                kvr_mc(mc)

        ktiles = st["kvr"]
        kvr_mixes(0, *ktiles[0])
        for tix, (t0, t1) in enumerate(ktiles):
            if tix + 1 < len(ktiles):
                kvr_mixes(tix + 1, *ktiles[tix + 1])
            kvr_tile(tix, t0, t1)
        if sti == 0:
            tap("rwk0", rwk[0].h[:, 0:512], [rwk[0].iv(0, 512)])

        mark(f"wo{sti}")
        arena.off = 0
        sqx = [arena.take(512, F32) for _ in range(2)]
        t1s = arena.take(512, F32)
        t1g = arena.take(512, F32)
        Rb = arena.take(512, F32)
        M2b = arena.take(512, F32)
        rows = {nm: arena.take(512, F32) for nm in ["ssq", "sx", "rstd", "m2"]}
        kkq = arena.take(16 * 512, BF16)
        kvacc = [Buf(rwk[mc].h[:, 0:1024], f"rwk{mc}", 0, 2, 1024)
                 for mc in range(NCH)]
        mxck = arena.take(8 * 1024, BF16)
        rls = [arena.take(512, BF16) for _ in range(2)]
        kvf = arena.take(512, F32)
        dsc3 = arena.take(512, BF16)
        mxcr = Buf(ar16[:, (kkq.off + 8192) // 2:(kkq.off + 8192) // 2 + 4096],
                   "ARENA", kkq.off + 8192, 2, 4096)
        dsc2 = arena.take(512, BF16)
        ntile = len(st["rt"])
        woS = wload(wdram["Wo"])
        wcrS = wload(wdram["Wcr"])
        ckw = {0: [wload(wdram["Wck"][0]), None]}

        def wo_mm_tile(g0, g1):
            L = g1 - g0
            bcol = g0 - dom

            def wo_mc(mc):
                bnk = pb[6 + mc % 2]

                def mm(kc):
                    E("tensor", lambda e: e.matmul(
                        bnk.h[:, 0:L],
                        woS.h[:, kc * 1024 + mc * 128:kc * 1024 + (mc + 1) * 128],
                        rwk[kc].h[:, bcol:bcol + L],
                        start=(kc == 0), stop=(kc == 7)),
                      reads=[woS.iv(kc * 1024, (kc + 1) * 1024),
                             rwk[kc].iv(bcol, bcol + L)],
                      writes=piv(bnk))

                for kc in range(NCH):
                    mm(kc)
                E("vector", lambda e: e.tensor_tensor(
                    XN[mc].h[:, bcol:bcol + L], bnk.h[:, 0:L],
                    XN[mc].h[:, bcol:bcol + L], OP.add),
                  reads=piv(bnk) + [XN[mc].iv(bcol, bcol + L)],
                  writes=[XN[mc].iv(bcol, bcol + L)])

            for mc in range(NCH):
                wo_mc(mc)

        def woW1(gix, g0, g1):
            L = g1 - g0
            bcol = g0 - dom

            def stats_kc(kc):
                sq = sqx[kc % 2]
                E("scalar", lambda e: e.activation(
                    sq.h[:, 0:L], XN[kc].h[:, bcol:bcol + L], AF.Square),
                  reads=[XN[kc].iv(bcol, bcol + L)], writes=[sq.iv()])
                E("tensor", lambda e: e.matmul(
                    pb[2].h[0:1, 0:L], onescol.h[:], sq.h[:, 0:L],
                    start=(kc == 0), stop=(kc == 7)),
                  reads=[sq.iv(), onescol.iv()], writes=piv(pb[2]))
                E("tensor", lambda e: e.matmul(
                    pb[3].h[0:1, 0:L], onescol_b.h[:],
                    XN[kc].h[:, bcol:bcol + L],
                    start=(kc == 0), stop=(kc == 7)),
                  reads=[XN[kc].iv(bcol, bcol + L), onescol_b.iv()],
                  writes=piv(pb[3]))

            for kc in range(NCH):
                stats_kc(kc)

        def woW2(gix, g0, g1):
            L = g1 - g0
            bcol = g0 - dom
            E("vector", lambda e: e.tensor_copy(rows["ssq"].h[0:1, 0:L],
                                                pb[2].h[0:1, 0:L]),
              reads=piv(pb[2]), writes=[rows["ssq"].iv()])
            E("vector", lambda e: e.tensor_copy(rows["sx"].h[0:1, 0:L],
                                                pb[3].h[0:1, 0:L]),
              reads=piv(pb[3]), writes=[rows["sx"].iv()])
            E("vector", lambda e: e.tensor_scalar(
                rows["sx"].h[0:1, 0:L], rows["sx"].h[0:1, 0:L], 1.0 / C, None,
                op0=OP.mult),
              reads=[rows["sx"].iv()], writes=[rows["sx"].iv()])
            E("vector", lambda e: e.tensor_tensor(
                rows["m2"].h[0:1, 0:L], rows["sx"].h[0:1, 0:L],
                rows["sx"].h[0:1, 0:L], OP.mult),
              reads=[rows["sx"].iv()], writes=[rows["m2"].iv()])
            E("vector", lambda e: e.scalar_tensor_tensor(
                rows["ssq"].h[0:1, 0:L], rows["ssq"].h[0:1, 0:L], 1.0 / C,
                rows["m2"].h[0:1, 0:L], op0=OP.mult, op1=OP.subtract),
              reads=[rows["ssq"].iv(), rows["m2"].iv()],
              writes=[rows["ssq"].iv()])
            E("scalar", lambda e: e.activation(
                rows["ssq"].h[0:1, 0:L], rows["ssq"].h[0:1, 0:L], AF.Ln,
                bias=epscol.h[0:1, 0:1]),
              reads=[rows["ssq"].iv(), epscol.iv()],
              writes=[rows["ssq"].iv()])
            E("scalar", lambda e: e.activation(
                rows["rstd"].h[0:1, 0:L], rows["ssq"].h[0:1, 0:L], AF.Exp,
                scale=-0.5),
              reads=[rows["ssq"].iv()], writes=[rows["rstd"].iv()])
            E("vector", lambda e: e.tensor_tensor(
                rows["m2"].h[0:1, 0:L], rows["sx"].h[0:1, 0:L],
                rows["rstd"].h[0:1, 0:L], OP.mult),
              reads=[rows["sx"].iv(), rows["rstd"].iv()],
              writes=[rows["m2"].iv()])
            E("tensor", lambda e: e.matmul(
                pb[4].h[:, 0:L], ones_f.h[:], rows["rstd"].h[0:1, 0:L],
                start=True, stop=True),
              reads=[rows["rstd"].iv(), ones_f.iv()], writes=piv(pb[4]))
            E("vector", lambda e: e.tensor_copy(Rb.h[:, 0:L], pb[4].h[:, 0:L]),
              reads=piv(pb[4]), writes=[Rb.iv()])
            E("tensor", lambda e: e.matmul(
                pb[5].h[:, 0:L], ones_f.h[:], rows["m2"].h[0:1, 0:L],
                start=True, stop=True),
              reads=[rows["m2"].iv(), ones_f.iv()], writes=piv(pb[5]))
            E("vector", lambda e: e.tensor_copy(M2b.h[:, 0:L], pb[5].h[:, 0:L]),
              reads=piv(pb[5]), writes=[M2b.iv()])

        def woW3(gix, g0, g1):
            L = g1 - g0
            bcol = g0 - dom

            def xc_kc(kc):
                eng = "vector"
                ts = t1s if kc % 2 else t1g
                E(eng, lambda e: e.tensor_tensor(
                    ts.h[:, 0:L], XN[kc].h[:, bcol:bcol + L], Rb.h[:, 0:L],
                    OP.mult),
                  reads=[XN[kc].iv(bcol, bcol + L), Rb.iv()], writes=[ts.iv()])
                E(eng, lambda e: e.tensor_tensor(
                    ts.h[:, 0:L], ts.h[:, 0:L], M2b.h[:, 0:L], OP.subtract),
                  reads=[ts.iv(), M2b.iv()], writes=[ts.iv()])
                E(eng, lambda e: e.tensor_scalar(
                    xcS[kc].h[:, bcol:bcol + L], ts.h[:, 0:L],
                    vecs["ln1_g"].h[:, kc:kc + 1],
                    vecs["ln1_b"].h[:, kc:kc + 1], op0=OP.mult, op1=OP.add),
                  reads=[ts.iv(), vecs["ln1_g"].iv(), vecs["ln1_b"].iv()],
                  writes=[xcS[kc].iv(bcol, bcol + L)])
                if sti == 0 and gix == 0:
                    E("vector", lambda e: e.tensor_tensor(
                        xcS[kc].h[:, 0:128], xcS[kc].h[:, 0:128],
                        m01b.h[:, 128:256], OP.mult),
                      reads=[xcS[kc].iv(0, 128), m01b.iv()],
                      writes=[xcS[kc].iv(0, 128)])

            for kc in range(NCH):
                xc_kc(kc)

        def cmix(dst, base, kc, bcol, firstC, cmvec, dscb):
            s0 = 1 if firstC else 0
            cur = xcS[kc].h[:, bcol + s0:bcol + 512]
            shf = xcS[kc].h[:, bcol + s0 - 1:bcol + 511]
            riv = [xcS[kc].iv(max(bcol - 1, 0), bcol + 512)]
            E("vector", lambda e: e.tensor_tensor(
                dscb.h[:, 0:512 - s0], cur, shf, OP.subtract),
              reads=riv, writes=[dscb.iv()])
            E("vector", lambda e: e.scalar_tensor_tensor(
                dst.h[:, base + s0:base + 512], dscb.h[:, 0:512 - s0],
                vecs[cmvec].h[:, kc:kc + 1], shf, op0=OP.mult, op1=OP.add),
              reads=[dscb.iv(), vecs[cmvec].iv()] + riv,
              writes=[dst.iv(base + s0, base + 512)])
            if firstC:
                E("vector", lambda e: e.tensor_tensor(
                    dscb.h[:, 0:1], xcS[kc].h[:, 0:1], xccar.h[:, kc:kc + 1],
                    OP.subtract),
                  reads=[xcS[kc].iv(0, 1), xccar.iv(kc, kc + 1)],
                  writes=[dscb.iv(0, 2)])
                E("vector", lambda e: e.scalar_tensor_tensor(
                    dst.h[:, base:base + 1], dscb.h[:, 0:1],
                    vecs[cmvec].h[:, kc:kc + 1], xccar.h[:, kc:kc + 1],
                    op0=OP.mult, op1=OP.add),
                  reads=[dscb.iv(0, 2), vecs[cmvec].iv(), xccar.iv(kc, kc + 1)],
                  writes=[dst.iv(base, base + 1)])

        def wcr_tile(tix, t0):
            bcol = t0 - dom
            firstC = (sti == 1 and tix == 0)
            for kc in range(NCH):
                cmix(mxcr, kc * 512, kc, bcol, firstC, "cm_r", dsc2)

            def wcr_mc(mc):
                bnk = pb[mc % 2]

                def mm(kc):
                    E("tensor", lambda e: e.matmul(
                        bnk.h[:, 0:512],
                        wcrS.h[:, kc * 1024 + mc * 128:kc * 1024 + (mc + 1) * 128],
                        mxcr.h[:, kc * 512:(kc + 1) * 512],
                        start=(kc == 0), stop=(kc == 7)),
                      reads=[wcrS.iv(kc * 1024, (kc + 1) * 1024),
                             mxcr.iv(kc * 512, (kc + 1) * 512)],
                      writes=piv(bnk))

                for kc in range(NCH):
                    mm(kc)
                E("scalar", lambda e: e.activation(
                    xaT[mc].h[:, tix * 512:(tix + 1) * 512], bnk.h[:, 0:512],
                    AF.Sigmoid),
                  reads=piv(bnk),
                  writes=[xaT[mc].iv(tix * 512, (tix + 1) * 512)])

            for mc in range(NCH):
                wcr_mc(mc)

        def ck_kk(q, tix):
            wslA = ckw[q][0]

            def kk_fl(fl):
                bnk = pb[fl % 2]

                def mm(kc):
                    E("tensor", lambda e: e.matmul(
                        bnk.h[:, 0:512],
                        wslA.h[:, kc * 1024 + fl * 128:kc * 1024 + (fl + 1) * 128],
                        mxck.h[:, kc * 1024 + tix * 512:kc * 1024 + (tix + 1) * 512],
                        start=(kc == 0), stop=(kc == 7)),
                      reads=[wslA.iv(kc * 1024, (kc + 1) * 1024),
                             mxck.iv(kc * 1024 + tix * 512,
                                     kc * 1024 + (tix + 1) * 512)],
                      writes=piv(bnk))

                for kc in range(NCH):
                    mm(kc)
                rl = rls[fl % 2]
                E("scalar", lambda e: e.activation(
                    rl.h[:, 0:512], bnk.h[:, 0:512], AF.Relu),
                  reads=piv(bnk), writes=[rl.iv()])
                kq = (tix * 8 + fl) * 512
                E("gpsimd", lambda e: e.tensor_tensor(
                    kkq.h[:, kq:kq + 512], rl.h[:, 0:512], rl.h[:, 0:512],
                    OP.mult),
                  reads=[rl.iv()], writes=[kkq.iv(kq, kq + 512)])

            for fl in range(8):
                kk_fl(fl)

        def ck_kv(q, tix):
            wslB = ckw[q][1]

            def kv_mc(mc):
                bnk = pb[2 + mc % 4]

                def mm(fl):
                    kq = (tix * 8 + fl) * 512
                    E("tensor", lambda e: e.matmul(
                        bnk.h[:, 0:512],
                        wslB.h[:, fl * 1024 + mc * 128:fl * 1024 + (mc + 1) * 128],
                        kkq.h[:, kq:kq + 512],
                        start=(fl == 0), stop=(fl == 7)),
                      reads=[wslB.iv(fl * 1024, (fl + 1) * 1024),
                             kkq.iv(kq, kq + 512)],
                      writes=piv(bnk))

                for fl in range(8):
                    mm(fl)
                acc = kvacc[mc]
                if q == 0:
                    E("vector", lambda e: e.tensor_copy(
                        acc.h[:, tix * 512:(tix + 1) * 512], bnk.h[:, 0:512]),
                      reads=piv(bnk), writes=[acc.iv(tix * 512, (tix + 1) * 512)])
                else:
                    E("vector", lambda e: e.tensor_tensor(
                        acc.h[:, tix * 512:(tix + 1) * 512], bnk.h[:, 0:512],
                        acc.h[:, tix * 512:(tix + 1) * 512], OP.add),
                      reads=piv(bnk) + [acc.iv(tix * 512, (tix + 1) * 512)],
                      writes=[acc.iv(tix * 512, (tix + 1) * 512)])

            for mc in range(NCH):
                kv_mc(mc)

        for g0_, g1_ in st["gt"]:
            wo_mm_tile(g0_, g1_)
        rt_emitted = [False] * ntile

        def emit_rt(gcov):
            for t_, (rt0_, rt1_) in enumerate(st["rt"]):
                if not rt_emitted[t_] and rt1_ <= gcov:
                    rt_emitted[t_] = True
                    firstC_ = (sti == 1 and t_ == 0)
                    wcr_tile(t_, rt0_)
                    for kc in range(NCH):
                        cmix(mxck, kc * 1024 + t_ * 512, kc, rt0_ - dom,
                             firstC_, "cm_k", dsc3)
                    ck_kk(0, t_)

        gts = st["gt"]
        woW1(0, *gts[0])
        woW2(0, *gts[0])
        for g_ in range(1, len(gts)):
            woW1(g_, *gts[g_])
            woW3(g_ - 1, *gts[g_ - 1])
            emit_rt(gts[g_ - 1][1])
            woW2(g_, *gts[g_])
        woW3(len(gts) - 1, *gts[-1])
        emit_rt(gts[-1][1])
        if sti == 0:
            tap("x1", XN[0].h[:, 0:512], [XN[0].iv(0, 512)])
            tap("xc0", xcS[0].h[:, 0:512], [xcS[0].iv(0, 512)])
            tap("rr0", xaT[0].h[:, 0:512], [xaT[0].iv(0, 512)])

            def xc_car(kc):
                E("vector", lambda e: e.tensor_copy(
                    xccar.h[:, kc:kc + 1], xcS[kc].h[:, 1151:1152]),
                  reads=[xcS[kc].iv(1151, 1152)], writes=[xccar.iv(kc, kc + 1)])
            for kc in range(NCH):
                xc_car(kc)

        ckw[0][1] = wload(wdram["Wcv"][0])
        for tix0 in range(ntile):
            ck_kv(0, tix0)
        for q in range(1, 4):
            ckw[q] = [wload(wdram["Wck"][q]), wload(wdram["Wcv"][q])]
            for tix0 in range(ntile):
                ck_kk(q, tix0)
            for tix0 in range(ntile):
                ck_kv(q, tix0)

        def s_tile(tix, t0):
            bcol = t0 - dom

            def s_mc(mc):
                E("vector", lambda e: e.tensor_tensor(
                    kvf.h[:, 0:512], kvacc[mc].h[:, tix * 512:(tix + 1) * 512],
                    xaT[mc].h[:, tix * 512:(tix + 1) * 512], OP.mult),
                  reads=[kvacc[mc].iv(tix * 512, (tix + 1) * 512),
                         xaT[mc].iv(tix * 512, (tix + 1) * 512)],
                  writes=[kvf.iv()])
                E("vector", lambda e: e.tensor_tensor(
                    XN[mc].h[:, bcol:bcol + 512], XN[mc].h[:, bcol:bcol + 512],
                    kvf.h[:, 0:512], OP.add),
                  reads=[XN[mc].iv(bcol, bcol + 512), kvf.iv()],
                  writes=[XN[mc].iv(bcol, bcol + 512)])

            for mc in range(NCH):
                s_mc(mc)

        for tix, (t0, t1) in enumerate(st["rt"]):
            s_tile(tix, t0)

        arena.off = 0
        sgF = [arena.take(512, F32) for _ in range(NCH)]
        xrw = [arena.take(1024, F32) for _ in range(6)]
        osb = [arena.take(1024, F32) for _ in range(2)]

        def out_tile(t0, tb):
            bcol = t0 - dom
            gcol = t0 - gdom

            def sg_mc(mc):
                E("vector", lambda e: e.tensor_tensor(
                    sgF[mc].h[:, 0:512], XN[mc].h[:, bcol:bcol + 512],
                    gS[mc].h[:, gcol:gcol + 512], OP.mult),
                  reads=[XN[mc].iv(bcol, bcol + 512),
                         gS[mc].iv(gcol, gcol + 512)],
                  writes=[sgF[mc].iv()])

            for mc in range(NCH):
                sg_mc(mc)

            def load_j(j):
                r0 = t0 + j * 128
                s = (tb + j) % 6
                E("sync", lambda e: e.dma_start(out=xrw[s].h[:],
                                                in_=xsl[r0:r0 + 128, :]),
                  writes=[xrw[s].iv()], dma=f"xr{s}")

            for j in range(4):
                load_j(j)

            def out_j(j):
                r0 = t0 + j * 128
                par = j % 2

                def tr_mc(mc):
                    bnk = pb[mc]
                    E("tensor", lambda e: e.transpose(
                        bnk.h[:, 0:128], sgF[mc].h[:, j * 128:(j + 1) * 128],
                        idn.h[:]),
                      reads=[sgF[mc].iv(j * 128, (j + 1) * 128), idn.iv()],
                      writes=piv(bnk))
                    E("vector", lambda e: e.tensor_tensor(
                        osb[par].h[:, mc * 128:(mc + 1) * 128], bnk.h[:, 0:128],
                        xrw[(tb + j) % 6].h[:, mc * 128:(mc + 1) * 128], OP.add),
                      reads=piv(bnk) + [xrw[(tb + j) % 6].iv()],
                      writes=[osb[par].iv(mc * 128, (mc + 1) * 128)])

                for mc in range(NCH):
                    tr_mc(mc)
                E("sync", lambda e: e.dma_start(
                    out=osl[r0 - HALO:r0 - HALO + 128, :], in_=osb[par].h[:]),
                  reads=[osb[par].iv()], writes=[], dma=f"ost{par}")

            for j in range(4):
                out_j(j)

        for tix0, (t0, t1) in enumerate(st["rt"]):
            out_tile(t0, tix0 * 4)

    for sti_, st_ in enumerate(STS):
        do_st(sti_, st_)

    dkeys = P.finalize()
    guaranteed = {e: {} for e in Prog.ENGS}
    with contextlib.ExitStack() as ctx:
        esems = {e: ctx.enter_context(nc.semaphore("s_" + e)) for e in Prog.ENGS}
        dsems = {k: ctx.enter_context(nc.semaphore("d_" + k)) for k in dkeys}
        blk = ctx.enter_context(nc.Block())

        @blk.tensor
        def _(t):
            P.replay("tensor", t, esems, dsems, guaranteed)

        @blk.vector
        def _(v):
            P.replay("vector", v, esems, dsems, guaranteed)

        @blk.scalar
        def _(s):
            P.replay("scalar", s, esems, dsems, guaranteed)

        @blk.gpsimd
        def _(g):
            P.replay("gpsimd", g, esems, dsems, guaranteed)

        @blk.sync
        def _(sy):
            P.replay("sync", sy, esems, dsems, guaranteed)
            for k in dkeys:
                sy.wait_ge(dsems[k], P.dtot[k] * 16)
    return nc


def _host_maps(inputs):
    x = np.ascontiguousarray(np.asarray(inputs["L0_attn_x"], dtype=np.float32))

    def f32(a):
        return np.asarray(a, dtype=np.float32)

    def sq(w):
        return np.ascontiguousarray(
            f32(w).reshape(8, 128, 1024).transpose(1, 0, 2).reshape(128, 8192))

    shared = {
        "Wk": sq(inputs["L0_attn_Wk"]), "Wv": sq(inputs["L0_attn_Wv"]),
        "Wr": sq(inputs["L0_attn_Wr"]), "Wo": sq(inputs["L0_attn_Wo"]),
        "Wcr": sq(inputs["L0_attn_Wcr"]),
        "Wck": np.ascontiguousarray(f32(inputs["L0_attn_Wck"]).reshape(
            8, 128, 4, 1024).transpose(2, 1, 0, 3).reshape(4, 128, 8192)),
        "Wcv": np.ascontiguousarray(f32(inputs["L0_attn_Wcv"]).reshape(
            4, 8, 128, 1024).transpose(0, 2, 1, 3).reshape(4, 128, 8192)),
        "Wffn": np.ascontiguousarray(f32(inputs["L0_ffn_W"]).reshape(
            8, 8, 128, 1024).transpose(0, 2, 1, 3).reshape(8, 128, 8192)),
        "bffn": inputs["L0_ffn_b"], "sigs": inputs["L0_ffn_tile_sigs"],
        "ln0_g": inputs["L0_attn_ln0_g"], "ln0_b": inputs["L0_attn_ln0_b"],
        "ln1_g": inputs["L0_attn_ln1_g"], "ln1_b": inputs["L0_attn_ln1_b"],
        "time_decay": inputs["L0_attn_time_decay"],
        "time_first": inputs["L0_attn_time_first"],
        "tm_k": inputs["L0_attn_tm_k"], "tm_v": inputs["L0_attn_tm_v"],
        "tm_r": inputs["L0_attn_tm_r"], "cm_k": inputs["L0_attn_cm_k"],
        "cm_r": inputs["L0_attn_cm_r"],
    }
    shared = {k: np.ascontiguousarray(np.asarray(v, dtype=np.float32))
              for k, v in shared.items()}
    in_maps = []
    for core in range(8):
        b = core // 4
        t0 = (core % 4) * REAL
        lo = t0 - HALO
        xs = np.zeros((EXT, C), np.float32)
        msk = np.zeros((1, HALO), np.float32)
        src_lo = max(lo, 0)
        xs[src_lo - lo:] = x[b, src_lo:t0 + REAL]
        msk[0, src_lo - lo:] = 1.0
        msk[0, 0] = 0.0
        lr = ((1.0 - msk) * LOGMASK).astype(np.float32)
        im = dict(shared)
        im["xsl"] = xs
        im["mrow"] = msk
        im["lrow"] = lr
        in_maps.append(im)
    return in_maps


def kernel(**inputs):
    nc = build_nc()
    in_maps = _host_maps(inputs)
    res = run_bass_kernel_spmd(nc, in_maps, core_ids=list(range(8)))
    out = np.zeros((2, 8192, C), np.float32)
    for core in range(8):
        b = core // 4
        t0 = (core % 4) * REAL
        out[b, t0:t0 + REAL] = res.results[core]["osl"]
    return out


def run_debug(inputs, taps):
    global TAPS_WANT
    TAPS_WANT = tuple(taps)
    try:
        nc = build_nc()
        in_maps = _host_maps(inputs)
        res = run_bass_kernel_spmd(nc, in_maps, core_ids=list(range(8)))
        out = np.zeros((2, 8192, C), np.float32)
        for core in range(8):
            b = core // 4
            t0 = (core % 4) * REAL
            out[b, t0:t0 + REAL] = res.results[core]["osl"]
        dbgs = [res.results[core].get("dbg") for core in range(8)]
        return out, dbgs
    finally:
        TAPS_WANT = ()
SERIAL = False
SERIAL_FROM = 0
SERIAL_TO = 10**9
ONLY_LOAD = False
PHASE_MARKS = {}


if __name__ == "__main__":
    import sys
    if "--build" in sys.argv:
        nc_ = build_nc()
        print("built ok; instructions:", nc_.n_instructions())
